# version 01
# baseline (speedup 1.0000x reference)
import math
from contextlib import ExitStack

import numpy as np
import concourse.bass as bass
import concourse.mybir as mybir
from concourse.bass import IndirectOffsetOnAxis
from concourse.bass_utils import run_bass_kernel_spmd

F32 = mybir.dt.float32
BF16 = mybir.dt.bfloat16
U32 = mybir.dt.uint32
AF = mybir.ActivationFunctionType
ALU = mybir.AluOpType

NCORES = 8
T = 16384
TC = T // NCORES
D = 1024
NCH = D // 128
H = 16
HD = 64

ENGINES = ("sync", "tensor", "vector", "scalar", "gpsimd")


class TT:

    def __init__(self, h, name):
        self.h = h
        self.name = name
        self.w = []
        self.r = []

    def __getitem__(self, sl):
        return self.h[sl]

    def ap(self):
        return self.h.ap() if hasattr(self.h, "ap") else self.h[:]


class Prog:
    def __init__(self):
        self.nc = bass.Bass("TRN2", target_bir_lowering=False)
        self.es = ExitStack()
        self.ops = {e: [] for e in ENGINES}
        self.sems = {}
        self.plan = {}
        self.watermark = {e: {} for e in ENGINES}
        self.pending_free = []
        self._cms = {}
        self.inputs = {}
        self.outputs = {}
        b0 = (self.nc.sbuf_base + 31) // 32 * 32
        self._sb_free = [(b0, self.nc.sbuf_top - 64)]
        self._sb_addr = {}
        self._sb_peak = 0
        self._psum_seq = []

    def din(self, name, shape, dt=F32):
        ap = self.nc.dram_tensor(name, list(shape), dt, kind="ExternalInput").ap()
        t = TT(ap, name)
        self.inputs[name] = t
        return t

    def dout(self, name, shape, dt=F32):
        ap = self.nc.dram_tensor(name, list(shape), dt, kind="ExternalOutput").ap()
        t = TT(ap, name)
        self.outputs[name] = t
        return t

    def dram(self, name, shape, dt=F32, addr_space="Local"):
        ap = self.nc.dram_tensor(name, list(shape), dt, kind="Internal",
                                 addr_space=addr_space).ap()
        return TT(ap, name)

    def sbuf(self, name, shape, dt=F32, hi=False):
        per_part = int(np.prod(shape[1:])) * mybir.dt.size(dt)
        need = (per_part + 31) // 32 * 32
        rng = reversed(list(enumerate(self._sb_free))) if hi \
            else enumerate(self._sb_free)
        for idx, (s, e) in rng:
            if e - s >= need:
                break
        else:
            livemap = sorted(self._sb_addr.items(), key=lambda kv: kv[1][0])
            lm = ", ".join(f"{n}@{a}+{b - a}" for n, (a, b) in livemap)
            raise MemoryError(f"SBUF OOM for {name}: need {need}B/partition; "
                              f"free={self._sb_free}; live: {lm}")
        s, e = self._sb_free[idx]
        if hi:
            a0 = e - need
            if a0 == s:
                del self._sb_free[idx]
            else:
                self._sb_free[idx] = (s, a0)
            s = a0
        elif e - s == need:
            del self._sb_free[idx]
        else:
            self._sb_free[idx] = (s + need, e)
        h = self.nc.alloc_sbuf_tensor_at(name, list(shape), dt, offset=s)
        self._sb_addr[name] = (s, s + need)
        self._sb_peak = max(self._sb_peak, sum(
            b - a for a, b in self._sb_addr.values()))
        t = TT(h, name)
        t.kind = "sbuf"
        t.r = list(self.pending_free)
        return t

    def psum(self, name, shape, dt=F32):
        cm = self.nc.psum_tensor(name, list(shape), dt)
        h = cm.__enter__()
        self._cms[name] = cm
        self._psum_seq.append(name)
        t = TT(h, name)
        t.kind = "psum"
        t.r = list(self.pending_free)
        return t

    def free(self, *tiles):
        psums = [t for t in tiles if getattr(t, "kind", None) == "psum"]
        psums.sort(key=lambda t: self._psum_seq.index(t.name), reverse=True)
        for t in psums:
            if self._psum_seq[-1] != t.name:
                raise RuntimeError(f"psum free not LIFO: {t.name} vs stack "
                                   f"{self._psum_seq}")
            self._psum_seq.pop()
            self.pending_free.extend(t.w + t.r)
            self._cms.pop(t.name).__exit__(None, None, None)
        for t in tiles:
            if getattr(t, "kind", None) != "sbuf":
                continue
            self.pending_free.extend(t.w + t.r)
            s, e = self._sb_addr.pop(t.name)
            self._sb_free.append((s, e))
        self._sb_free.sort()
        merged = []
        for s, e in self._sb_free:
            if merged and merged[-1][1] == s:
                merged[-1] = (merged[-1][0], e)
            else:
                merged.append((s, e))
        self._sb_free = [tuple(x) for x in merged]
        best = {}
        for (e_, s_, v_) in self.pending_free:
            if s_ not in best or best[s_][2] < v_:
                best[s_] = (e_, s_, v_)
        self.pending_free = list(best.values())

    def _sem(self, name):
        if name not in self.sems:
            self.sems[name] = self.es.enter_context(self.nc.semaphore(name))
            self.plan[name] = 0
        return self.sems[name]

    _Q_ISSUER = {"q_sync": "sync", "q_gpsimd": "gpsimd", "q_scalar": "scalar",
                 "cc": "gpsimd"}

    def _dma_barrier(self, s, v):
        issuer = self._Q_ISSUER[s]
        bs = "b_" + s
        if not hasattr(self, "_bar"):
            self._bar = {}
        cover, cnt = self._bar.get(s, (0, 0))
        if cover >= v:
            return bs, cnt
        total = self.plan[s]
        self._sem(s)
        self._sem(bs)
        sems = self.sems
        self.ops[issuer].append(
            lambda e, s=s, total=total: e.wait_ge(sems[s], total))
        self.ops[issuer].append(lambda e, bs=bs: e.sem_inc(sems[bs], 1))
        cnt += 1
        self.plan[bs] = cnt
        self._bar[s] = (total, cnt)
        self.watermark[issuer][s] = max(self.watermark[issuer].get(s, 0), total)
        return bs, cnt

    def op(self, eng, build, reads=(), writes=(), dma=False, pe_chain=False,
           inc=None, after=()):
        reads = [t for t in reads if t is not None]
        writes = [t for t in writes if t is not None]
        want = list(after)
        for t in reads:
            want += t.w
        for t in writes:
            want += t.w + t.r
        need = {}
        for (e, s, v) in want:
            if pe_chain and e == "tensor" and eng == "tensor":
                continue
            if s.startswith("q_") or s == "cc":
                s, v = self._dma_barrier(s, v)
            if self.watermark[eng].get(s, 0) >= v:
                continue
            if need.get(s, 0) < v:
                need[s] = v
        if inc is None:
            semname = ("q_" + eng) if dma else ("c_" + eng)
            n = 16 if dma else 1
        else:
            semname, n = inc
        self._sem(semname)
        for s in need:
            self._sem(s)
        self.plan[semname] += n
        tok = (eng, semname, self.plan[semname])
        waits = sorted(need.items())
        sems = self.sems

        def run(e, waits=waits, build=build, semname=semname, n=n):
            for s, v in waits:
                e.wait_ge(sems[s], v)
            i = build(e)
            i.then_inc(sems[semname], n)

        self.ops[eng].append(run)
        for s, v in waits:
            self.watermark[eng][s] = max(self.watermark[eng].get(s, 0), v)
        for t in reads:
            t.r.append(tok)
        for t in writes:
            t.w = [tok]
            t.r = []
        return tok

    def raw_wait(self, eng, tiles):
        want = []
        for t in tiles:
            want += t.w
        need = {}
        for (e, s, v) in want:
            if self.watermark[eng].get(s, 0) >= v:
                continue
            if need.get(s, 0) < v:
                need[s] = v
        sems = self.sems
        for s, v in sorted(need.items()):
            self._sem(s)
            self.ops[eng].append(lambda e, s=s, v=v: e.wait_ge(sems[s], v))
            self.watermark[eng][s] = max(self.watermark[eng].get(s, 0), v)

    def dma(self, q, dst, dst_sl, src, src_sl):
        return self.op(q, lambda e: e.dma_start(out=dst.h[dst_sl], in_=src.h[src_sl]),
                       reads=[src], writes=[dst], dma=True)

    def mm(self, out, osl, lhsT, lsl, rhs, rsl, start=True, stop=True,
           skip_group_check=False):
        def _base(sl):
            s = sl[0].start if isinstance(sl, tuple) else None
            return int(s or 0)
        tp = (_base(lsl), _base(osl))
        return self.op("tensor",
                       lambda e: e.matmul(out.h[osl], lhsT.h[lsl], rhs.h[rsl],
                                          start=start, stop=stop,
                                          tile_position=tp,
                                          skip_group_check=skip_group_check),
                       reads=[lhsT, rhs], writes=[out], pe_chain=True)

    def finalize_output_waits(self):
        toks = []
        for t in self.outputs.values():
            toks += t.w
        need = {}
        for (e, s, v) in toks:
            if need.get(s, 0) < v:
                need[s] = v
        sems = self.sems
        for s, v in sorted(need.items()):
            self.ops["gpsimd"].append(lambda e, s=s, v=v: e.wait_ge(sems[s], v))

    def build(self):
        nc = self.nc
        with nc.Block() as block:
            @block.sync
            def _(eng):
                for fn in self.ops["sync"]:
                    fn(eng)

            @block.tensor
            def _(eng):
                for fn in self.ops["tensor"]:
                    fn(eng)

            @block.vector
            def _(eng):
                for fn in self.ops["vector"]:
                    fn(eng)

            @block.scalar
            def _(eng):
                for fn in self.ops["scalar"]:
                    fn(eng)

            @block.gpsimd
            def _(eng):
                for fn in self.ops["gpsimd"]:
                    fn(eng)
        self.es.close()
        return nc



def emit_consts(p):
    c = {}
    c["ones_col"] = p.sbuf("ones_col", [128, 1], BF16)
    c["ones_row"] = p.sbuf("ones_row", [1, 128], BF16)
    p.op("gpsimd", lambda e: e.memset(c["ones_col"][:, :], 1.0), writes=[c["ones_col"]])
    p.op("gpsimd", lambda e: e.memset(c["ones_row"][:, :], 1.0), writes=[c["ones_row"]])
    return c


def emit_ln(p, c, x8, T_loc, name, out_dram=None, make_bf=True, eps=1e-5):
    nt = T_loc // 512
    ps_s = p.psum(name + "_pss", [1, T_loc], F32)
    ps_q = p.psum(name + "_psq", [1, T_loc], F32)
    xb = [p.sbuf(name + f"_xb{i}", [128, T_loc], BF16, hi=True) for i in range(2)]
    sq = [p.sbuf(name + f"_sq{i}", [128, T_loc], BF16, hi=True) for i in range(2)]
    for k in range(NCH):
        b = xb[k % 2]
        s = sq[k % 2]
        p.op("vector", lambda e, k=k, b=b: e.tensor_copy(out=b[:, :], in_=x8[k][:, :]),
             reads=[x8[k]], writes=[b])
        p.op("scalar", lambda e, k=k, s=s: e.activation(out=s[:, :], in_=x8[k][:, :],
                                                        func=AF.Square),
             reads=[x8[k]], writes=[s])
        for tt in range(nt):
            sl = slice(tt * 512, tt * 512 + 512)
            p.mm(ps_s, (slice(0, 1), sl), c["ones_col"], (slice(None), slice(None)),
                 b, (slice(None), sl), start=(k == 0), stop=(k == NCH - 1))
            p.mm(ps_q, (slice(0, 1), sl), c["ones_col"], (slice(None), slice(None)),
                 s, (slice(None), sl), start=(k == 0), stop=(k == NCH - 1))
    mu = p.sbuf(name + "_mu", [1, T_loc], F32)
    var = p.sbuf(name + "_var", [1, T_loc], F32)
    rstd = p.sbuf(name + "_rstd", [1, T_loc], F32)
    p.op("vector", lambda e: e.tensor_scalar_mul(out=mu[:, :], in0=ps_s[0:1, :],
                                                 scalar1=1.0 / D),
         reads=[ps_s], writes=[mu])
    p.op("vector", lambda e: e.tensor_scalar_mul(out=var[:, :], in0=ps_q[0:1, :],
                                                 scalar1=1.0 / D),
         reads=[ps_q], writes=[var])
    musq = p.sbuf(name + "_musq", [1, T_loc], F32)
    p.op("vector", lambda e: e.tensor_mul(out=musq[:, :], in0=mu[:, :], in1=mu[:, :]),
         reads=[mu], writes=[musq])
    p.op("vector", lambda e: e.tensor_sub(out=var[:, :], in0=var[:, :], in1=musq[:, :]),
         reads=[var, musq], writes=[var])
    lnv = p.sbuf(name + "_lnv", [1, T_loc], F32)
    p.op("vector", lambda e: e.tensor_scalar_add(out=var[:, :], in0=var[:, :],
                                                 scalar1=float(eps)),
         reads=[var], writes=[var])
    p.op("scalar", lambda e: e.activation(out=lnv[:, :], in_=var[:, :], func=AF.Ln),
         reads=[var], writes=[lnv])
    p.op("vector", lambda e: e.tensor_scalar_mul(out=lnv[:, :], in0=lnv[:, :],
                                                 scalar1=-0.5),
         reads=[lnv], writes=[lnv])
    p.op("scalar", lambda e: e.activation(out=rstd[:, :], in_=lnv[:, :], func=AF.Exp),
         reads=[lnv], writes=[rstd])
    p.free(ps_s, ps_q, musq, lnv, xb[0], xb[1], sq[0], sq[1])
    ps_bc = p.psum(name + "_psbc", [128, 1024], F32)
    mur = p.sbuf(name + "_mur", [1, T_loc], BF16)
    rsr = p.sbuf(name + "_rsr", [1, T_loc], BF16)
    p.op("scalar", lambda e: e.copy(out=mur[:, :], in_=mu[:, :]), reads=[mu], writes=[mur])
    p.op("scalar", lambda e: e.copy(out=rsr[:, :], in_=rstd[:, :]), reads=[rstd], writes=[rsr])
    xtmp = p.sbuf(name + "_xtmp", [128, 512], F32)
    for tt in range(nt):
        sl = slice(tt * 512, tt * 512 + 512)
        p.mm(ps_bc, (slice(None), slice(0, 512)), c["ones_row"],
             (slice(0, 1), slice(None)), mur, (slice(0, 1), sl))
        p.mm(ps_bc, (slice(None), slice(512, 1024)), c["ones_row"],
             (slice(0, 1), slice(None)), rsr, (slice(0, 1), sl))
        for k in range(NCH):
            p.op("vector", lambda e, k=k, sl=sl: e.tensor_sub(
                out=xtmp[:, :], in0=x8[k][:, sl], in1=ps_bc[:, 0:512]),
                reads=[x8[k], ps_bc], writes=[xtmp])
            p.op("vector", lambda e, k=k, sl=sl: e.tensor_mul(
                out=x8[k][:, sl], in0=xtmp[:, :], in1=ps_bc[:, 512:1024]),
                reads=[xtmp, ps_bc], writes=[x8[k]])
    p.free(ps_bc, xtmp, mu, var, rstd, mur, rsr)
    xnb8 = None
    if make_bf:
        xnb8 = [p.sbuf(name + f"_xnb{k}", [128, T_loc], BF16) for k in range(NCH)]
        for k in range(NCH):
            p.op("vector", lambda e, k=k: e.tensor_copy(out=xnb8[k][:, :],
                                                        in_=x8[k][:, :]),
                 reads=[x8[k]], writes=[xnb8[k]])
    if out_dram is not None:
        for k in range(NCH):
            p.dma("sync", out_dram, (slice(k * 128, k * 128 + 128), slice(None)),
                  x8[k], (slice(None), slice(None)))
    return xnb8


def emit_l0_attn(p, c, xnb8, io, upto='full'):
    nt = TC // 512
    ncnk = TC // 128
    qp = [p.sbuf(f"qp{a}", [128, TC], BF16) for a in range(NCH)]
    kp = [p.sbuf(f"kp{a}", [128, TC], BF16) for a in range(NCH)]
    wq = [p.sbuf(f"wq{k}", [128, 1024], BF16, hi=True) for k in range(NCH)]
    wkv = [p.sbuf(f"wkv{k}", [128, 2048], BF16) for k in range(NCH)]
    for k in range(NCH):
        p.dma("gpsimd", wq[k], (slice(None), slice(None)),
              io["qkvw"], (slice(k * 128, k * 128 + 128), slice(0, 1024)))
        p.dma("gpsimd", wkv[k], (slice(None), slice(None)),
              io["qkvw"], (slice(k * 128, k * 128 + 128), slice(1024, 3072)))
    qkvb_c = p.sbuf("qkvb_c", [128, 24], F32)
    p.dma("sync", qkvb_c, (slice(None), slice(None)), io["qkvb_col"],
          (slice(None), slice(None)))
    qkvb_r = p.sbuf("qkvb_r", [1, 2048], F32, hi=True)
    p.dma("sync", qkvb_r, (slice(None), slice(None)), io["qkvb_row"],
          (slice(None), slice(1024, 3072)))
    qkvb_rb = p.sbuf("qkvb_rb", [1, 2048], BF16)
    p.op("vector", lambda e: e.tensor_copy(out=qkvb_rb[:, :], in_=qkvb_r[:, :]),
         reads=[qkvb_r], writes=[qkvb_rb])
    p.free(qkvb_r)
    cmask = p.sbuf("cmask", [128, 128], BF16)
    p.dma("sync", cmask, (slice(None), slice(None)), io["cmask"],
          (slice(None), slice(None)))
    pref = p.sbuf("pref", [128, 8], F32)
    p.dma("sync", pref, (slice(None), slice(None)), io["prefmask"],
          (slice(None), slice(None)))
    parind = p.sbuf("parind", [128, 2], BF16)
    p.dma("sync", parind, (slice(None), slice(None)), io["parind"],
          (slice(None), slice(None)))

    psqk = p.psum("psqk", [128, 512], F32)
    e_scr = p.sbuf("e_scr", [128, 512], BF16, hi=True)
    for j in range(16):
        dest = qp[j] if j < 8 else kp[j - 8]
        wmat = wq if j < 8 else wkv
        jc = j if j < 8 else j - 8
        for tt in range(nt):
            tsl = slice(tt * 512, tt * 512 + 512)
            for k in range(NCH):
                p.mm(psqk, (slice(None), slice(None)),
                     wmat[k], (slice(None), slice(jc * 128, jc * 128 + 128)),
                     xnb8[k], (slice(None), tsl),
                     start=(k == 0), stop=(k == NCH - 1))
            bsl = slice(j, j + 1)
            p.op("scalar", lambda e, bsl=bsl: e.activation(
                out=e_scr[:, :], in_=psqk[:, :], func=AF.Exp, bias=qkvb_c[:, bsl]),
                reads=[psqk, qkvb_c], writes=[e_scr])
            p.op("scalar", lambda e, bsl=bsl, dest=dest, tsl=tsl: e.activation(
                out=dest[:, tsl], in_=psqk[:, :], func=AF.Relu, bias=qkvb_c[:, bsl]),
                reads=[psqk, qkvb_c], writes=[dest])
            p.op("vector", lambda e, dest=dest, tsl=tsl: e.scalar_tensor_tensor(
                out=dest[:, tsl], in0=e_scr[:, :], scalar=1.0, in1=dest[:, tsl],
                op0=ALU.min, op1=ALU.add),
                reads=[e_scr, dest], writes=[dest])
    p.free(psqk, e_scr)
    for k in range(NCH):
        p.free(wq[k])
    if upto == 'qk':
        return

    vtm = p.sbuf("vtm", [128, ncnk * 1024], BF16)
    ktm = p.sbuf("ktm", [128, ncnk * 1024], BF16)
    psv = p.psum("psv", [128, 1024], F32)
    psk = p.psum("psk", [128, 1024], F32)
    ek_scr = p.sbuf("ek_scr", [128, 1024], BF16, hi=True)
    for i in range(ncnk):
        isl = slice(i * 128, i * 128 + 128)
        for half in range(2):
            osl = slice(half * 512, half * 512 + 512)
            esl = slice(1024 + half * 512, 1024 + half * 512 + 512)
            ksl = slice(half * 512, half * 512 + 512)
            for k in range(NCH):
                p.mm(psv, (slice(None), osl), xnb8[k], (slice(None), isl),
                     wkv[k], (slice(None), esl), start=(k == 0), stop=False)
                p.mm(psk, (slice(None), osl), xnb8[k], (slice(None), isl),
                     wkv[k], (slice(None), ksl), start=(k == 0), stop=False)
            p.mm(psv, (slice(None), osl), c["ones_row"], (slice(0, 1), slice(None)),
                 qkvb_rb, (slice(0, 1), slice(1024 + half * 512, 1024 + half * 512 + 512)),
                 start=False, stop=True)
            p.mm(psk, (slice(None), osl), c["ones_row"], (slice(0, 1), slice(None)),
                 qkvb_rb, (slice(0, 1), slice(half * 512, half * 512 + 512)),
                 start=False, stop=True)
        p.op("vector", lambda e, i=i: e.tensor_copy(
            out=vtm[:, i * 1024:(i + 1) * 1024], in_=psv[:, :]),
            reads=[psv], writes=[vtm])
        kosl = slice(i * 1024, i * 1024 + 1024)
        p.op("scalar", lambda e: e.activation(out=ek_scr[:, :], in_=psk[:, :],
                                              func=AF.Exp),
             reads=[psk], writes=[ek_scr])
        p.op("scalar", lambda e, kosl=kosl: e.activation(
            out=ktm[:, kosl], in_=psk[:, :], func=AF.Relu),
            reads=[psk], writes=[ktm])
        p.op("vector", lambda e, kosl=kosl: e.scalar_tensor_tensor(
            out=ktm[:, kosl], in0=ek_scr[:, :], scalar=1.0, in1=ktm[:, kosl],
            op0=ALU.min, op1=ALU.add),
            reads=[ek_scr, ktm], writes=[ktm])
    p.free(psv, psk, ek_scr)
    for k in range(NCH):
        p.free(wkv[k], xnb8[k])
    if upto == 'tm':
        return

    psA = p.psum("psA", [128, 2048], F32)
    p.op("vector", lambda e: e.memset(psA[:, :], 0.0), writes=[psA])
    for h in range(H):
        a, hh = h // 2, h % 2
        for i in range(ncnk):
            ksl = (slice(None), slice(i * 1024 + h * 64, i * 1024 + h * 64 + 64))
            vsl = (slice(None), slice(i * 1024 + h * 64, i * 1024 + h * 64 + 64))
            osl = (slice(hh * 64, hh * 64 + 64), slice(a * 128, a * 128 + 64))
            p.mm(psA, osl, ktm, ksl, vtm, vsl,
                 start=(i == 0), stop=False)
            zsl = (slice(hh * 64, hh * 64 + 64), slice(a * 128 + 64, a * 128 + 65))
            p.mm(psA, zsl, ktm, ksl, c["ones_col"], (slice(None), slice(None)),
                 start=False, stop=(i == ncnk - 1))
    st_sb = p.sbuf("st_sb", [128, 1024], F32)
    p.op("vector", lambda e: e.tensor_copy(out=st_sb[:, :], in_=psA[:, 0:1024]),
         reads=[psA], writes=[st_sb])
    p.dma("sync", io["ag_in"], (slice(None), slice(None)), st_sb,
          (slice(None), slice(None)))
    p.op("gpsimd", lambda e: e.collective_compute(
        "AllGather", ALU.bypass, replica_groups=[list(range(NCORES))],
        ins=[io["ag_in"].h[:, :]], outs=[io["ag_out"].h[:, :]]),
        reads=[io["ag_in"]], writes=[io["ag_out"]], inc=("cc", 1))
    s0 = p.sbuf("s0", [128, 1024], F32)
    s0b = p.sbuf("s0b", [128, 1024], BF16)
    tmpag = p.sbuf("tmpag", [128, 1024], F32)
    p.op("vector", lambda e: e.memset(s0[:, :], 0.0), writes=[s0])
    for j in range(NCORES):
        p.dma("sync", tmpag, (slice(None), slice(None)), io["ag_out"],
              (slice(j * 128, j * 128 + 128), slice(None)))
        p.op("vector", lambda e, j=j: e.scalar_tensor_tensor(
            out=s0[:, :], in0=tmpag[:, :], scalar=pref[:, j:j + 1], in1=s0[:, :],
            op0=ALU.mult, op1=ALU.add),
            reads=[tmpag, pref, s0], writes=[s0])
    p.free(tmpag, st_sb)
    p.op("vector", lambda e: e.tensor_copy(out=s0b[:, :], in_=s0[:, :]),
         reads=[s0], writes=[s0b])
    if upto == 'ag':
        return

    ctxA = p.psum("ctxA", [128, 2048], F32)
    ctx_fm = [p.sbuf(f"ctxfm{a}", [128, TC], BF16) for a in range(NCH)]
    amt = p.sbuf("amt", [128, 2048], BF16)
    zq = p.sbuf("zq", [128, 128], BF16)
    nrm = p.sbuf("nrm", [1, 2048], F32)
    rn_rowb = p.sbuf("rn_rowb", [1, 2048], BF16)
    rnb = p.sbuf("rnb", [128, 2048], BF16)
    nloop = 1 if (upto == 'loop1' or upto.startswith('c_')) else ncnk
    for i in range(nloop):
        csl = slice(i * 128, i * 128 + 128)
        for h in [0, 2, 4, 6, 8, 10, 12, 14, 1, 3, 5, 7, 9, 11, 13, 15]:
            if upto == 'c_at_even' and h % 2 == 1:
                continue
            a, hh = h // 2, h % 2
            rows = slice(hh * 64, hh * 64 + 64)
            p.mm(psA, (slice(None), slice(h * 128, h * 128 + 128)),
                 kp[a], (rows, csl), qp[a], (rows, csl))
        if upto in ('c_at', 'c_at_even'):
            return
        mb = cmask.h[:, :].unsqueeze(1).broadcast_to([128, 16, 128])
        amt3 = amt.h[:, :].rearrange("p (h x) -> p h x", x=128)
        psA3 = psA.h[:, :].rearrange("p (h x) -> p h x", x=128)
        p.op("vector", lambda e, mb=mb, amt3=amt3, psA3=psA3: e.tensor_tensor(
            out=amt3, in0=psA3, in1=mb, op=ALU.mult),
            reads=[psA, cmask], writes=[amt])
        if upto == 'c_amt':
            return
        for h in range(H):
            a, hh = h // 2, h % 2
            rows = slice(hh * 64, hh * 64 + 64)
            hsl = slice(h * 128, h * 128 + 128)
            p.mm(ctxA, (rows, hsl),
                 vtm, (slice(None), slice(i * 1024 + h * 64, i * 1024 + h * 64 + 64)),
                 amt, (slice(None), hsl), start=True, stop=False)
            p.mm(ctxA, (rows, hsl),
                 s0b, (rows, slice(a * 128, a * 128 + 64)),
                 qp[a], (rows, csl), start=False, stop=True)
        if upto == 'c_ctx':
            return
        for qq in range(4):
            qsl = slice(qq * 512, qq * 512 + 512)
            p.mm(psA, (slice(0, 1), slice(1024, 1536)), c["ones_col"],
                 (slice(None), slice(None)), amt, (slice(None), qsl),
                 start=True, stop=False)
            for aq in range(2):
                a = qq * 2 + aq
                p.op("vector", lambda e, a=a, csl=csl: e.tensor_scalar(
                    out=zq[:, :], in0=qp[a][:, csl],
                    scalar1=s0[:, a * 128 + 64:a * 128 + 65], scalar2=None,
                    op0=ALU.mult),
                    reads=[qp[a], s0], writes=[zq])
                for hh in range(2):
                    hq = aq * 2 + hh
                    p.mm(psA, (slice(0, 1),
                               slice(1024 + hq * 128, 1024 + hq * 128 + 128)),
                         parind, (slice(None), slice(hh, hh + 1)),
                         zq, (slice(None), slice(None)),
                         start=False, stop=(hq == 3))
            p.op("vector", lambda e, qsl=qsl: e.tensor_scalar_add(
                out=nrm[0:1, qsl], in0=psA[0:1, 1024:1536], scalar1=1e-6),
                reads=[psA], writes=[nrm])
        p.op("vector", lambda e: e.reciprocal(out=nrm[:, :], in_=nrm[:, :]),
             reads=[nrm], writes=[nrm])
        p.op("scalar", lambda e: e.copy(out=rn_rowb[:, :], in_=nrm[:, :]),
             reads=[nrm], writes=[rn_rowb])
        if upto == 'c_nrm':
            return
        for h in range(H):
            a, hh = h // 2, h % 2
            ksl = (slice(None), slice(i * 1024 + h * 64, i * 1024 + h * 64 + 64))
            p.mm(psA, (slice(hh * 64, hh * 64 + 64),
                       slice(a * 128, a * 128 + 64)),
                 ktm, ksl,
                 vtm, (slice(None), slice(i * 1024 + h * 64, i * 1024 + h * 64 + 64)),
                 start=True, stop=False)
            p.mm(psA, (slice(hh * 64, hh * 64 + 64),
                       slice(a * 128 + 64, a * 128 + 65)),
                 ktm, ksl, c["ones_col"], (slice(None), slice(None)),
                 start=False, stop=True)
        p.op("vector", lambda e: e.tensor_add(out=s0[:, :], in0=s0[:, :],
                                              in1=psA[:, 0:1024]),
             reads=[s0, psA], writes=[s0])
        p.op("vector", lambda e: e.tensor_copy(out=s0b[:, :], in_=s0[:, :]),
             reads=[s0], writes=[s0b])
        if upto == 'c_supd':
            return
        for qq in range(4):
            qsl = slice(qq * 512, qq * 512 + 512)
            p.mm(psA, (slice(None), slice(1536, 2048)), c["ones_row"],
                 (slice(0, 1), slice(None)), rn_rowb, (slice(0, 1), qsl))
            p.op("vector", lambda e, qsl=qsl: e.tensor_copy(
                out=rnb[:, qsl], in_=psA[:, 1536:2048]),
                reads=[psA], writes=[rnb])
        if upto == 'c_bcast':
            return
        for h in range(H):
            a, hh = h // 2, h % 2
            hsl = slice(h * 128, h * 128 + 128)
            rows = slice(hh * 64, hh * 64 + 64)
            p.op("vector", lambda e, a=a, hsl=hsl, rows=rows, csl=csl:
                 e.tensor_tensor(out=ctx_fm[a][rows, csl], in0=ctxA[rows, hsl],
                                 in1=rnb[rows, hsl], op=ALU.mult),
                 reads=[ctxA, rnb], writes=[ctx_fm[a]])
    p.free(psA, ctxA, amt, nrm, rn_rowb, rnb, s0, s0b, vtm, ktm)
    for a in range(NCH):
        p.free(qp[a], kp[a])
    if upto in ('loop1', 'loop'):
        return

    owb = [p.sbuf(f"owb{k}", [128, 1024], BF16) for k in range(NCH)]
    for k in range(NCH):
        p.dma("gpsimd", owb[k], (slice(None), slice(None)), io["outw"],
              (slice(k * 128, k * 128 + 128), slice(None)))
    outb_c = p.sbuf("outb_c", [128, 8], F32)
    p.dma("sync", outb_c, (slice(None), slice(None)), io["outb_col"],
          (slice(None), slice(None)))
    pso = p.psum("pso", [128, 512], F32)
    a0buf = p.sbuf("a0buf", [128, TC], F32)
    for o in range(NCH):
        for tt in range(nt):
            tsl = slice(tt * 512, tt * 512 + 512)
            for k in range(NCH):
                p.mm(pso, (slice(None), slice(None)),
                     owb[k], (slice(None), slice(o * 128, o * 128 + 128)),
                     ctx_fm[k], (slice(None), tsl),
                     start=(k == 0), stop=(k == NCH - 1))
            p.op("scalar", lambda e, tsl=tsl, o=o: e.add(
                out=a0buf[:, tsl], in_=pso[:, :], add=outb_c[:, o:o + 1]),
                reads=[pso, outb_c], writes=[a0buf])
        p.dma("sync", io["a0_o"], (slice(o * 128, o * 128 + 128), slice(None)),
              a0buf, (slice(None), slice(None)))
    p.free(pso, a0buf)
    for k in range(NCH):
        p.free(owb[k], ctx_fm[k])


def emit_l0_moe(p, c, io, upto="full"):
    ncnk = TC // 128
    CAP = 3072
    rg = [list(range(NCORES))]

    iota8 = p.sbuf("iota8", [128, 8], F32)
    p.dma("sync", iota8, (slice(None), slice(None)), io["iota8"], (slice(None), slice(None)))
    lts = p.sbuf("lts", [128, 128], BF16)
    p.dma("sync", lts, (slice(None), slice(None)), io["lts"], (slice(None), slice(None)))
    idbf = p.sbuf("idbf", [128, 128], BF16)
    p.dma("sync", idbf, (slice(None), slice(None)), io["idbf"], (slice(None), slice(None)))
    ut9 = p.sbuf("ut9", [8, 9], F32)
    p.dma("sync", ut9, (slice(None), slice(None)), io["ut9"], (slice(None), slice(None)))
    i8 = p.sbuf("i8", [8, 8], F32)
    p.dma("sync", i8, (slice(None), slice(None)), io["i8"], (slice(None), slice(None)))
    mrowc = p.sbuf("mrowc", [8, 8], F32)
    p.dma("sync", mrowc, (slice(None), slice(None)), io["maskrowc"], (slice(None), slice(None)))
    mlte = p.sbuf("mlte", [8, 8], F32)
    p.dma("sync", mlte, (slice(None), slice(None)), io["masklte"], (slice(None), slice(None)))
    pcol8 = p.sbuf("pcol8", [8, 1], F32)
    p.dma("sync", pcol8, (slice(None), slice(None)), io["prefcol8"], (slice(None), slice(None)))
    rwt = p.sbuf("rwt", [128, 64], F32)
    p.dma("sync", rwt, (slice(None), slice(None)), io["rwt8"], (slice(None), slice(None)))
    ones_row_f = p.sbuf("ones_row_f", [1, 128], F32)
    p.op("gpsimd", lambda e: e.memset(ones_row_f[:, :], 1.0), writes=[ones_row_f])
    coff = p.sbuf("coff", [1, 8], F32)
    p.dma("sync", coff, (slice(None), slice(None)), io["coff_row"], (slice(None), slice(None)))

    xn8 = [p.sbuf(f"bxn{k}", [128, TC]) for k in range(NCH)]
    xnb8 = [p.sbuf(f"bxnb{k}", [128, TC], BF16) for k in range(NCH)]
    for k in range(NCH):
        p.dma("sync", xn8[k], (slice(None), slice(None)), io["xn0_i"],
              (slice(k * 128, k * 128 + 128), slice(None)))
        p.op("vector", lambda e, k=k: e.tensor_copy(out=xnb8[k][:, :], in_=xn8[k][:, :]),
             reads=[xn8[k]], writes=[xnb8[k]])

    pslog = p.psum("pslog", [128, 8], F32)
    pscnt = p.psum("pscnt", [1, 8], F32)
    logit_sb = p.sbuf("logit_sb", [128, 8], F32)
    vmax = p.sbuf("vmax", [128, 8], F32)
    midx = p.sbuf("midx", [128, 8], U32)
    esub = p.sbuf("esub", [128, 8], F32)
    eexp = p.sbuf("eexp", [128, 8], F32)
    esum = p.sbuf("esum", [128, 1], F32)
    echo = p.sbuf("echo", [128, ncnk], F32)
    pstar = p.sbuf("pstar", [128, ncnk], F32)
    oh_f = p.sbuf("oh_f", [128, 8], F32)
    oh_bf = p.sbuf("oh_bf", [128, 8 * ncnk], BF16)
    for i in range(ncnk):
        csl = slice(i * 128, i * 128 + 128)
        for k in range(NCH):
            p.mm(pslog, (slice(None), slice(None)), xn8[k], (slice(None), csl),
                 rwt, (slice(None), slice(k * 8, k * 8 + 8)),
                 start=(k == 0), stop=(k == NCH - 1))
        p.op("vector", lambda e: e.tensor_copy(out=logit_sb[:, :], in_=pslog[:, :]),
             reads=[pslog], writes=[logit_sb])
        p.op("vector", lambda e: e.max(out=vmax[:, :], in_=logit_sb[:, :]),
             reads=[logit_sb], writes=[vmax])
        p.op("vector", lambda e: e.max_index(out=midx[:, :], in_max=vmax[:, :],
                                             in_values=logit_sb[:, :]),
             reads=[logit_sb, vmax], writes=[midx])
        p.op("vector", lambda e, i=i: e.tensor_copy(out=echo[:, i:i + 1],
                                                    in_=midx[:, 0:1]),
             reads=[midx], writes=[echo])
        p.op("vector", lambda e: e.tensor_scalar(
            out=esub[:, :], in0=logit_sb[:, :], scalar1=vmax[:, 0:1], scalar2=None,
            op0=ALU.subtract),
            reads=[logit_sb, vmax], writes=[esub])
        p.op("scalar", lambda e: e.activation(out=eexp[:, :], in_=esub[:, :],
                                              func=AF.Exp),
             reads=[esub], writes=[eexp])
        p.op("vector", lambda e: e.reduce_sum(out=esum[:, :], in_=eexp[:, :],
                                              axis=mybir.AxisListType.X),
             reads=[eexp], writes=[esum])
        p.op("vector", lambda e, i=i: e.reciprocal(out=pstar[:, i:i + 1],
                                                   in_=esum[:, :]),
             reads=[esum], writes=[pstar])
        p.op("vector", lambda e, i=i: e.tensor_scalar(
            out=oh_f[:, :], in0=iota8[:, :], scalar1=echo[:, i:i + 1], scalar2=None,
            op0=ALU.is_equal),
            reads=[iota8, echo], writes=[oh_f])
        p.op("vector", lambda e, i=i: e.tensor_copy(
            out=oh_bf[:, i * 8:i * 8 + 8], in_=oh_f[:, :]),
            reads=[oh_f], writes=[oh_bf])
        p.mm(pscnt, (slice(0, 1), slice(None)), c["ones_col"],
             (slice(None), slice(None)), oh_bf, (slice(None), slice(i * 8, i * 8 + 8)),
             start=(i == 0), stop=(i == ncnk - 1))
    counts_row = p.sbuf("counts_row", [1, 8], F32)
    p.op("vector", lambda e: e.tensor_copy(out=counts_row[:, :], in_=pscnt[0:1, :]),
         reads=[pscnt], writes=[counts_row])
    p.free(pslog, pscnt, logit_sb, vmax, midx, esub, eexp, esum, oh_f)
    for k in range(NCH):
        p.free(xn8[k])
    if upto == "b_route1":
        return

    p.dma("sync", io["ag2_in"], (slice(None), slice(None)), counts_row,
          (slice(None), slice(None)))
    p.op("gpsimd", lambda e: e.collective_compute(
        "AllGather", ALU.bypass, replica_groups=rg,
        ins=[io["ag2_in"].h[:, :]], outs=[io["ag2_out"].h[:, :]]),
        reads=[io["ag2_in"]], writes=[io["ag2_out"]], inc=("cc", 1))
    counts_sb = p.sbuf("counts_sb", [8, 8], F32)
    p.dma("sync", counts_sb, (slice(None), slice(None)), io["ag2_out"],
          (slice(None), slice(None)))
    psr = p.psum("psr", [128, 64], F32)
    pst_bf = p.psum("pst_bf", [128, 1024], BF16)
    p.mm(psr, (slice(0, 1), slice(0, 8)), pcol8, (slice(None), slice(None)),
         counts_sb, (slice(None), slice(None)))
    p.mm(psr, (slice(0, 8), slice(8, 9)), counts_row, (slice(0, 1), slice(None)),
         ones_row_f, (slice(0, 1), slice(0, 1)))
    gpfx_row = p.sbuf("gpfx_row", [1, 8], F32)
    ccol = p.sbuf("ccol", [8, 1], F32)
    p.op("vector", lambda e: e.tensor_copy(out=gpfx_row[:, :], in_=psr[0:1, 0:8]),
         reads=[psr], writes=[gpfx_row])
    p.op("vector", lambda e: e.tensor_copy(out=ccol[:, :], in_=psr[0:8, 8:9]),
         reads=[psr], writes=[ccol])
    p.mm(psr, (slice(0, 1), slice(16, 24)), ccol, (slice(None), slice(None)),
         ut9, (slice(None), slice(0, 8)))
    lstart_row = p.sbuf("lstart_row", [1, 8], F32)
    p.op("vector", lambda e: e.tensor_copy(out=lstart_row[:, :], in_=psr[0:1, 16:24]),
         reads=[psr], writes=[lstart_row])
    tmp88 = p.sbuf("tmp88", [8, 8], F32)
    cntc = p.sbuf("cntc", [8, 1], F32)
    locc = p.sbuf("locc", [8, 1], F32)
    p.op("vector", lambda e: e.tensor_mul(out=tmp88[:, :], in0=counts_sb[:, :],
                                          in1=mrowc[:, :]),
         reads=[counts_sb, mrowc], writes=[tmp88])
    p.op("vector", lambda e: e.reduce_sum(out=cntc[:, :], in_=tmp88[:, :],
                                          axis=mybir.AxisListType.X),
         reads=[tmp88], writes=[cntc])
    p.op("vector", lambda e: e.tensor_mul(out=tmp88[:, :], in0=counts_sb[:, :],
                                          in1=mlte[:, :]),
         reads=[counts_sb, mlte], writes=[tmp88])
    p.op("vector", lambda e: e.reduce_sum(out=locc[:, :], in_=tmp88[:, :],
                                          axis=mybir.AxisListType.X),
         reads=[tmp88], writes=[locc])
    p.mm(psr, (slice(0, 1), slice(24, 33)), cntc, (slice(None), slice(None)),
         ut9, (slice(None), slice(None)))
    p.mm(psr, (slice(0, 1), slice(40, 48)), locc, (slice(None), slice(None)),
         i8, (slice(None), slice(None)))
    gstart_row = p.sbuf("gstart_row", [1, 9], F32)
    locc_row = p.sbuf("locc_row", [1, 8], F32)
    p.op("vector", lambda e: e.tensor_copy(out=gstart_row[:, :], in_=psr[0:1, 24:33]),
         reads=[psr], writes=[gstart_row])
    p.op("vector", lambda e: e.tensor_copy(out=locc_row[:, :], in_=psr[0:1, 40:48]),
         reads=[psr], writes=[locc_row])
    base_row = p.sbuf("base_row", [1, 8], F32)
    p.op("vector", lambda e: e.tensor_sub(out=base_row[:, :], in0=locc_row[:, :],
                                          in1=gstart_row[0:1, 0:8]),
         reads=[locc_row, gstart_row], writes=[base_row])
    p.op("vector", lambda e: e.tensor_add(out=base_row[:, :], in0=base_row[:, :],
                                          in1=coff[:, :]),
         reads=[base_row, coff], writes=[base_row])
    delta_row = p.sbuf("delta_row", [1, 8], F32)
    p.op("vector", lambda e: e.tensor_sub(out=delta_row[:, :], in0=gpfx_row[:, :],
                                          in1=lstart_row[:, :]),
         reads=[gpfx_row, lstart_row], writes=[delta_row])
    p.mm(psr, (slice(None), slice(0, 8)), ones_row_f, (slice(0, 1), slice(None)),
         delta_row, (slice(0, 1), slice(None)))
    p.mm(psr, (slice(None), slice(16, 25)), ones_row_f, (slice(0, 1), slice(None)),
         gstart_row, (slice(0, 1), slice(None)))
    p.mm(psr, (slice(None), slice(32, 40)), ones_row_f, (slice(0, 1), slice(None)),
         base_row, (slice(0, 1), slice(None)))
    delta_bc = p.sbuf("delta_bc", [128, 8], F32)
    gstart_bc = p.sbuf("gstart_bc", [128, 9], F32)
    base_bc = p.sbuf("base_bc", [128, 8], F32)
    p.op("vector", lambda e: e.tensor_copy(out=delta_bc[:, :], in_=psr[:, 0:8]),
         reads=[psr], writes=[delta_bc])
    p.op("vector", lambda e: e.tensor_copy(out=gstart_bc[:, :], in_=psr[:, 16:25]),
         reads=[psr], writes=[gstart_bc])
    p.op("vector", lambda e: e.tensor_copy(out=base_bc[:, :], in_=psr[:, 32:40]),
         reads=[psr], writes=[base_bc])
    if upto == "b_hist":
        p.free(psr)
        return

    psLT = p.psum("psLT", [128, 8], F32)
    psD = p.psum("psD", [128, 8], F32)
    pscol2 = p.psum("pscol2", [1, 8], F32)
    runloc = p.sbuf("runloc", [1, 8], F32)
    p.op("vector", lambda e: e.tensor_copy(out=runloc[:, :], in_=lstart_row[:, :]),
         reads=[lstart_row], writes=[runloc])
    runbc = p.sbuf("runbc", [128, 8], F32)
    sp8 = p.sbuf("sp8", [128, 8], F32)
    sendpos = p.sbuf("sendpos", [128, 1], F32)
    sendu = p.sbuf("sendu", [128, 1], U32)
    globpos = p.sbuf("globpos", [128, 1], F32)
    kept = p.sbuf("kept", [128, 1], F32)
    keptp = p.sbuf("keptp", [128, ncnk], F32)
    retf = p.sbuf("retf", [128, ncnk], F32)
    xtm = p.sbuf("xtm", [128, 1024], BF16)
    for i in range(ncnk):
        csl = slice(i * 128, i * 128 + 128)
        ohsl = slice(i * 8, i * 8 + 8)
        p.mm(psLT, (slice(None), slice(None)), lts, (slice(None), slice(None)),
             oh_bf, (slice(None), ohsl))
        p.mm(psD, (slice(None), slice(None)), ones_row_f, (slice(0, 1), slice(None)),
             runloc, (slice(0, 1), slice(None)))
        p.op("vector", lambda e: e.tensor_copy(out=runbc[:, :], in_=psD[:, :]),
             reads=[psD], writes=[runbc])
        p.op("vector", lambda e: e.tensor_add(out=sp8[:, :], in0=psLT[:, :],
                                              in1=runbc[:, :]),
             reads=[psLT, runbc], writes=[sp8])
        p.op("vector", lambda e, ohsl=ohsl: e.tensor_tensor(
            out=sp8[:, :], in0=sp8[:, :], in1=oh_bf[:, ohsl], op=ALU.mult),
            reads=[sp8, oh_bf], writes=[sp8])
        p.op("vector", lambda e: e.reduce_sum(out=sendpos[:, :], in_=sp8[:, :],
                                              axis=mybir.AxisListType.X),
             reads=[sp8], writes=[sendpos])
        p.op("vector", lambda e: e.tensor_copy(out=sendu[:, :], in_=sendpos[:, :]),
             reads=[sendpos], writes=[sendu])
        p.mm(pscol2, (slice(0, 1), slice(None)), c["ones_col"],
             (slice(None), slice(None)), oh_bf, (slice(None), ohsl))
        p.op("vector", lambda e: e.tensor_add(out=runloc[:, :], in0=runloc[:, :],
                                              in1=pscol2[0:1, :]),
             reads=[runloc, pscol2], writes=[runloc])
        p.op("vector", lambda e, ohsl=ohsl: e.tensor_tensor(
            out=sp8[:, :], in0=delta_bc[:, :], in1=oh_bf[:, ohsl], op=ALU.mult),
            reads=[delta_bc, oh_bf], writes=[sp8])
        p.op("vector", lambda e: e.reduce_sum(out=globpos[:, :], in_=sp8[:, :],
                                              axis=mybir.AxisListType.X),
             reads=[sp8], writes=[globpos])
        p.op("vector", lambda e: e.tensor_add(out=globpos[:, :], in0=globpos[:, :],
                                              in1=sendpos[:, :]),
             reads=[globpos, sendpos], writes=[globpos])
        p.op("vector", lambda e: e.tensor_scalar(
            out=kept[:, :], in0=globpos[:, :], scalar1=float(CAP), scalar2=None,
            op0=ALU.is_lt),
            reads=[globpos], writes=[kept])
        p.op("vector", lambda e, i=i: e.tensor_tensor(
            out=keptp[:, i:i + 1], in0=kept[:, :], in1=pstar[:, i:i + 1],
            op=ALU.mult),
            reads=[kept, pstar], writes=[keptp])
        p.op("vector", lambda e, i=i: e.tensor_scalar(
            out=retf[:, i:i + 1], in0=echo[:, i:i + 1], scalar1=float(CAP),
            scalar2=None, op0=ALU.mult),
            reads=[echo], writes=[retf])
        p.op("vector", lambda e, i=i: e.tensor_add(
            out=retf[:, i:i + 1], in0=retf[:, i:i + 1], in1=globpos[:, :]),
            reads=[retf, globpos], writes=[retf])
        p.op("vector", lambda e, i=i: e.tensor_scalar(
            out=retf[:, i:i + 1], in0=retf[:, i:i + 1],
            scalar1=float(8 * CAP - 1), scalar2=None, op0=ALU.min),
            reads=[retf], writes=[retf])
        for k in range(NCH):
            p.op("tensor", lambda e, k=k, csl=csl: e.transpose(
                pst_bf[:, k * 128:k * 128 + 128], xnb8[k][:, csl], idbf[:, :]),
                reads=[xnb8[k], idbf], writes=[pst_bf], pe_chain=True)
        p.op("vector", lambda e: e.tensor_copy(out=xtm[:, :], in_=pst_bf[:, :]),
             reads=[pst_bf], writes=[xtm])
        p.op("gpsimd", lambda e: e.indirect_dma_start(
            out=io["send"].h[:, :], out_offset=IndirectOffsetOnAxis(sendu[:, :], 0),
            in_=xtm[:, :], in_offset=None),
            reads=[xtm, sendu], writes=[io["send"]], dma=True)
    p.free(psLT, psD, pscol2, pst_bf, sp8, sendpos, sendu, globpos, kept,
           runbc, runloc)
    for k in range(NCH):
        p.free(xnb8[k])
    if upto == "b_disp":
        return

    p.op("gpsimd", lambda e: e.collective_compute(
        "AllGather", ALU.bypass, replica_groups=rg,
        ins=[io["send"].h[:, :]], outs=[io["agtok"].h[:, :]]),
        reads=[io["send"]], writes=[io["agtok"]], inc=("cc", 1))

    p3072 = p.sbuf("p3072", [128, 24], F32)
    p.dma("sync", p3072, (slice(None), slice(None)), io["p3072"], (slice(None), slice(None)))
    gidx_f = p.sbuf("gidx_f", [128, 24], F32)
    gidx_u = p.sbuf("gidx_u", [128, 24], U32)
    stepm = p.sbuf("stepm", [128, 24], F32)
    p.op("vector", lambda e: e.tensor_scalar(
        out=gidx_f[:, :], in0=p3072[:, :], scalar1=base_bc[:, 0:1], scalar2=None,
        op0=ALU.add),
        reads=[p3072, base_bc], writes=[gidx_f])
    dbase = p.sbuf("dbase", [128, 7], F32)
    p.op("vector", lambda e: e.tensor_sub(out=dbase[:, :], in0=base_bc[:, 1:8],
                                          in1=base_bc[:, 0:7]),
         reads=[base_bc], writes=[dbase])
    for cc in range(1, 8):
        p.op("vector", lambda e, cc=cc: e.tensor_scalar(
            out=stepm[:, :], in0=p3072[:, :], scalar1=gstart_bc[:, cc:cc + 1],
            scalar2=None, op0=ALU.is_ge),
            reads=[p3072, gstart_bc], writes=[stepm])
        p.op("vector", lambda e, cc=cc: e.scalar_tensor_tensor(
            out=gidx_f[:, :], in0=stepm[:, :], scalar=dbase[:, cc - 1:cc],
            in1=gidx_f[:, :], op0=ALU.mult, op1=ALU.add),
            reads=[stepm, dbase, gidx_f], writes=[gidx_f])
    p.op("vector", lambda e: e.tensor_scalar(
        out=gidx_f[:, :], in0=gidx_f[:, :], scalar1=float(T - 1), scalar2=0.0,
        op0=ALU.min, op1=ALU.max),
        reads=[gidx_f], writes=[gidx_f])
    p.op("vector", lambda e: e.tensor_copy(out=gidx_u[:, :], in_=gidx_f[:, :]),
         reads=[gidx_f], writes=[gidx_u])
    p.free(psr, stepm, dbase, p3072)
    if upto == "b_gidx":
        return

    w1b = [p.sbuf(f"w1b{k}", [128, 4096], BF16) for k in range(NCH)]
    for k in range(NCH):
        p.dma("gpsimd", w1b[k], (slice(None), slice(None)), io["w1"],
              (slice(k * 128, k * 128 + 128), slice(None)))
    w2b = [p.sbuf(f"w2b{k}", [128, 1024], BF16) for k in range(32)]
    for k in range(32):
        p.dma("gpsimd", w2b[k], (slice(None), slice(None)), io["w2"],
              (slice(k * 128, k * 128 + 128), slice(None)))
    b1c = p.sbuf("b1c", [128, 32], F32)
    p.dma("sync", b1c, (slice(None), slice(None)), io["b1_col"], (slice(None), slice(None)))
    b2c = p.sbuf("b2c", [128, 8], F32)
    p.dma("sync", b2c, (slice(None), slice(None)), io["b2_col"], (slice(None), slice(None)))

    psg = p.psum("psg", [128, 1024], BF16)
    psh = p.psum("psh", [128, 512], F32)
    pse = p.psum("pse", [128, 512], F32)
    xg_tm = p.sbuf("xg_tm", [128, 4 * 1024], BF16)
    xg_fm = [p.sbuf(f"xg_fm{k}", [128, 512], BF16) for k in range(NCH)]
    htile = [p.sbuf(f"htile{f}", [128, 512], BF16) for f in range(32)]
    eo_fm = [p.sbuf(f"eo_fm{o}", [128, 512], BF16) for o in range(NCH)]
    eo_tm = p.sbuf("eo_tm", [128, 4 * 1024], BF16)
    for tile in range(6):
        for j in range(4):
            col = tile * 4 + j
            p.op("gpsimd", lambda e, j=j, col=col: e.indirect_dma_start(
                out=xg_tm[:, j * 1024:j * 1024 + 1024], out_offset=None,
                in_=io["agtok"].h[:, :],
                in_offset=IndirectOffsetOnAxis(gidx_u[:, col:col + 1], 0)),
                reads=[io["agtok"], gidx_u], writes=[xg_tm], dma=True)
        for k in range(NCH):
            for j in range(4):
                p.op("tensor", lambda e, k=k, j=j: e.transpose(
                    psg[:, j * 128:j * 128 + 128],
                    xg_tm[:, j * 1024 + k * 128:j * 1024 + k * 128 + 128],
                    idbf[:, :]),
                    reads=[xg_tm, idbf], writes=[psg], pe_chain=True)
            p.op("vector", lambda e, k=k: e.tensor_copy(
                out=xg_fm[k][:, :], in_=psg[:, 0:512]),
                reads=[psg], writes=[xg_fm[k]])
        for f in range(32):
            for k in range(NCH):
                p.mm(psh, (slice(None), slice(None)),
                     w1b[k], (slice(None), slice(f * 128, f * 128 + 128)),
                     xg_fm[k], (slice(None), slice(None)),
                     start=(k == 0), stop=(k == NCH - 1))
            p.op("scalar", lambda e, f=f: e.activation(
                out=htile[f][:, :], in_=psh[:, :], func=AF.Gelu,
                bias=b1c[:, f:f + 1]),
                reads=[psh, b1c], writes=[htile[f]])
        for o in range(NCH):
            for f in range(32):
                p.mm(pse, (slice(None), slice(None)),
                     w2b[f], (slice(None), slice(o * 128, o * 128 + 128)),
                     htile[f], (slice(None), slice(None)),
                     start=(f == 0), stop=(f == 31))
            p.op("scalar", lambda e, o=o: e.add(
                out=eo_fm[o][:, :], in_=pse[:, :], add=b2c[:, o:o + 1]),
                reads=[pse, b2c], writes=[eo_fm[o]])
        for j in range(4):
            for o in range(NCH):
                p.op("tensor", lambda e, j=j, o=o: e.transpose(
                    psg[:, o * 128:o * 128 + 128],
                    eo_fm[o][:, j * 128:j * 128 + 128], idbf[:, :]),
                    reads=[eo_fm[o], idbf], writes=[psg], pe_chain=True)
            p.op("vector", lambda e, j=j: e.tensor_copy(
                out=eo_tm[:, j * 1024:j * 1024 + 1024], in_=psg[:, :]),
                reads=[psg], writes=[eo_tm])
        eo_out_ap = io["eo"].h[tile * 512:tile * 512 + 512, :].rearrange(
            "(j q) f -> q j f", q=128)
        eo_in_ap = eo_tm.h[:, :].rearrange("p (j f) -> p j f", f=1024)
        p.op("sync", lambda e, o=eo_out_ap, s=eo_in_ap: e.dma_start(out=o, in_=s),
             reads=[eo_tm], writes=[io["eo"]], dma=True)
    for k in range(NCH):
        p.free(w1b[k], xg_fm[k], eo_fm[k])
    for k in range(32):
        p.free(w2b[k], htile[k])
    p.free(psg, psh, pse, xg_tm, eo_tm, b1c, b2c, gidx_u, gidx_f)
    if upto == "b_expert":
        return

    p.op("gpsimd", lambda e: e.collective_compute(
        "AllGather", ALU.bypass, replica_groups=rg,
        ins=[io["eo"].h[:, :]], outs=[io["ageo"].h[:, :]]),
        reads=[io["eo"]], writes=[io["ageo"]], inc=("cc", 1))
    x18 = [p.sbuf(f"x1_{k}", [128, TC]) for k in range(NCH)]
    for k in range(NCH):
        p.dma("sync", x18[k], (slice(None), slice(None)), io["x_i"],
              (slice(k * 128, k * 128 + 128), slice(None)))
    a0s = p.sbuf("a0s", [128, TC], F32)
    for k in range(NCH):
        p.dma("sync", a0s, (slice(None), slice(None)), io["a0_i"],
              (slice(k * 128, k * 128 + 128), slice(None)))
        p.op("vector", lambda e, k=k: e.tensor_add(out=x18[k][:, :],
                                                   in0=x18[k][:, :], in1=a0s[:, :]),
             reads=[x18[k], a0s], writes=[x18[k]])
    p.free(a0s)
    retu = p.sbuf("retu", [128, 1], U32)
    eo_g = p.sbuf("eo_g", [128, 1024], BF16)
    eo_s = p.sbuf("eo_s", [128, 1024], BF16)
    psf = p.psum("psf", [128, 1024], BF16)
    for i in range(ncnk):
        csl = slice(i * 128, i * 128 + 128)
        p.op("vector", lambda e, i=i: e.tensor_copy(out=retu[:, :],
                                                    in_=retf[:, i:i + 1]),
             reads=[retf], writes=[retu])
        p.op("gpsimd", lambda e: e.indirect_dma_start(
            out=eo_g[:, :], out_offset=None, in_=io["ageo"].h[:, :],
            in_offset=IndirectOffsetOnAxis(retu[:, :], 0)),
            reads=[io["ageo"], retu], writes=[eo_g], dma=True)
        p.op("vector", lambda e, i=i: e.tensor_scalar(
            out=eo_s[:, :], in0=eo_g[:, :], scalar1=keptp[:, i:i + 1], scalar2=None,
            op0=ALU.mult),
            reads=[eo_g, keptp], writes=[eo_s])
        for k in range(NCH):
            p.op("tensor", lambda e, k=k: e.transpose(
                psf[:, k * 128:k * 128 + 128], eo_s[:, k * 128:k * 128 + 128],
                idbf[:, :]),
                reads=[eo_s, idbf], writes=[psf], pe_chain=True)
        for k in range(NCH):
            p.op("vector", lambda e, k=k, csl=csl: e.tensor_add(
                out=x18[k][:, csl], in0=x18[k][:, csl],
                in1=psf[:, k * 128:k * 128 + 128]),
                reads=[x18[k], psf], writes=[x18[k]])
    p.free(psf, retu, eo_g, eo_s, echo, pstar, keptp, retf, oh_bf, xtm)
    for k in range(NCH):
        p.dma("sync", io["x1_o"], (slice(k * 128, k * 128 + 128), slice(None)),
              x18[k], (slice(None), slice(None)))
    emit_ln(p, c, x18, TC, "ln1", out_dram=io["xn1_o"], make_bf=False)
    for k in range(NCH):
        p.free(x18[k])



def build_launch_A(upto='full'):
    p = Prog()
    io = {}
    x_fm = p.din("x_fm", [D, TC])
    io["qkvw"] = p.din("qkvw", [D, 3072])
    io["qkvb_col"] = p.din("qkvb_col", [128, 24])
    io["qkvb_row"] = p.din("qkvb_row", [1, 3072])
    io["outw"] = p.din("outw", [D, D])
    io["outb_col"] = p.din("outb_col", [128, 8])
    io["prefmask"] = p.din("prefmask", [128, 8])
    io["cmask"] = p.din("cmask", [128, 128], BF16)
    io["parind"] = p.din("parind", [128, 2], BF16)
    io["ag_in"] = p.dram("ag_in", [128, 1024])
    io["ag_out"] = p.dram("ag_out", [128 * NCORES, 1024])
    xn0_o = p.dout("xn0_o", [D, TC])
    io["a0_o"] = p.dout("a0_o", [D, TC])

    c = emit_consts(p)
    x8 = [p.sbuf(f"x{k}", [128, TC]) for k in range(NCH)]
    for k in range(NCH):
        p.dma("sync", x8[k], (slice(None), slice(None)), x_fm,
              (slice(k * 128, k * 128 + 128), slice(None)))
    xnb8 = emit_ln(p, c, x8, TC, "ln0", out_dram=xn0_o, make_bf=True)
    for k in range(NCH):
        p.free(x8[k])
    if upto != 'ln':
        emit_l0_attn(p, c, xnb8, io, upto=upto)
    p.finalize_output_waits()
    return p.build()


def emit_l1_attn(p, c, io, upto="full"):
    ncnk = TC // 128
    rg = [list(range(NCORES))]
    nt = TC // 512

    sm = {}
    for nm, w in (("tmr8", 8), ("tmk8", 8), ("tmv8", 8), ("decay8", 8),
                  ("first8", 8), ("gnw8", 8), ("gnb8", 8), ("kvec", 8),
                  ("selmask", 8), ("parf", 2)):
        sm[nm] = p.sbuf("c_" + nm, [128, w], F32)
        p.dma("sync", sm[nm], (slice(None), slice(None)), io[nm],
              (slice(None), slice(None)))
    e2 = p.sbuf("c_e2", [2, 128], F32)
    p.dma("sync", e2, (slice(None), slice(None)), io["e2"], (slice(None), slice(None)))

    xe = [p.sbuf(f"c_xe{k}", [128, TC + 1]) for k in range(NCH)]
    for k in range(NCH):
        p.dma("sync", xe[k], (slice(None), slice(1, TC + 1)), io["xn1_i"],
              (slice(k * 128, k * 128 + 128), slice(None)))
    lc = p.sbuf("c_lc", [128, 8], F32)
    for k in range(NCH):
        p.op("vector", lambda e, k=k: e.tensor_copy(out=lc[:, k:k + 1],
                                                    in_=xe[k][:, TC:TC + 1]),
             reads=[xe[k]], writes=[lc])
    p.dma("sync", io["ag4_in"], (slice(None), slice(None)), lc,
          (slice(None), slice(None)))
    p.op("gpsimd", lambda e: e.collective_compute(
        "AllGather", ALU.bypass, replica_groups=rg,
        ins=[io["ag4_in"].h[:, :]], outs=[io["ag4_out"].h[:, :]]),
        reads=[io["ag4_in"]], writes=[io["ag4_out"]], inc=("cc", 1))
    lcag = p.sbuf("c_lcag", [128, 8], F32)
    prevc = p.sbuf("c_prevc", [128, 8], F32)
    p.op("vector", lambda e: e.memset(prevc[:, :], 0.0), writes=[prevc])
    for j in range(NCORES):
        p.dma("sync", lcag, (slice(None), slice(None)), io["ag4_out"],
              (slice(j * 128, j * 128 + 128), slice(None)))
        p.op("vector", lambda e, j=j: e.scalar_tensor_tensor(
            out=prevc[:, :], in0=lcag[:, :], scalar=sm["selmask"][:, j:j + 1],
            in1=prevc[:, :], op0=ALU.mult, op1=ALU.add),
            reads=[lcag, sm["selmask"], prevc], writes=[prevc])
    for k in range(NCH):
        p.op("vector", lambda e, k=k: e.tensor_copy(out=xe[k][:, 0:1],
                                                    in_=prevc[:, k:k + 1]),
             reads=[prevc], writes=[xe[k]])
    p.free(lc, lcag, prevc)

    mixes = {}
    dscr = p.sbuf("c_dscr", [128, TC])
    for nm, tmn in (("xr", "tmr8"), ("xk", "tmk8"), ("xv", "tmv8")):
        mixes[nm] = [p.sbuf(f"c_{nm}{k}", [128, TC], BF16) for k in range(NCH)]
        for k in range(NCH):
            p.op("vector", lambda e, k=k: e.tensor_sub(
                out=dscr[:, :], in0=xe[k][:, 1:TC + 1], in1=xe[k][:, 0:TC]),
                reads=[xe[k]], writes=[dscr])
            p.op("vector", lambda e, k=k, nm=nm, tmn=tmn: e.scalar_tensor_tensor(
                out=mixes[nm][k][:, :], in0=dscr[:, :], scalar=sm[tmn][:, k:k + 1],
                in1=xe[k][:, 0:TC], op0=ALU.mult, op1=ALU.add),
                reads=[dscr, sm[tmn], xe[k]], writes=[mixes[nm][k]])
    p.free(dscr)
    for k in range(NCH):
        p.free(xe[k])
    if upto == "c1_mix":
        return

    psm = p.psum("c_psm", [128, 512], F32)
    rkv = {}
    for wname, xin, act in (("wr", "xr", "sig"), ("wk", "xk", "exp"),
                            ("wv", "xv", "copy")):
        dest = [p.sbuf(f"c_d{wname}{j}", [128, TC], BF16) for j in range(NCH)]
        rkv[wname] = dest
        wb = [p.sbuf(f"c_{wname}{k}", [128, 1024], BF16, hi=True) for k in range(NCH)]
        for k in range(NCH):
            p.dma("gpsimd", wb[k], (slice(None), slice(None)), io[wname],
                  (slice(k * 128, k * 128 + 128), slice(None)))
        xm = mixes[xin]
        for j in range(NCH):
            for tt in range(nt):
                tsl = slice(tt * 512, tt * 512 + 512)
                for k in range(NCH):
                    p.mm(psm, (slice(None), slice(None)),
                         wb[k], (slice(None), slice(j * 128, j * 128 + 128)),
                         xm[k], (slice(None), tsl),
                         start=(k == 0), stop=(k == NCH - 1))
                if act == "sig":
                    p.op("scalar", lambda e, d=dest[j], tsl=tsl: e.activation(
                        out=d[:, tsl], in_=psm[:, :], func=AF.Sigmoid),
                        reads=[psm], writes=[dest[j]])
                elif act == "exp":
                    p.op("scalar", lambda e, d=dest[j], tsl=tsl: e.activation(
                        out=d[:, tsl], in_=psm[:, :], func=AF.Exp),
                        reads=[psm], writes=[dest[j]])
                else:
                    p.op("vector", lambda e, d=dest[j], tsl=tsl: e.tensor_copy(
                        out=d[:, tsl], in_=psm[:, :]),
                        reads=[psm], writes=[dest[j]])
        for k in range(NCH):
            p.free(wb[k], xm[k])
    p.free(psm)
    rsig, ek, vv = rkv["wr"], rkv["wk"], rkv["wv"]
    if upto == "c1_rkv":
        if "dbg_o" in io:
            for k in range(NCH):
                for nm, src_t in (("r", rsig[k]), ("k", ek[k]), ("v", vv[k])):
                    pass
            dbg = io["dbg_o"]
            scr = p.sbuf("c_dbg", [128, TC], F32)
            for idx, grp in enumerate((rsig, ek, vv)):
                for k in range(NCH):
                    p.op("vector", lambda e, g=grp[k]: e.tensor_copy(
                        out=scr[:, :], in_=g[:, :]), reads=[grp[k]], writes=[scr])
                    p.dma("sync", dbg, (slice((idx * 8 + k) * 128,
                                              (idx * 8 + k) * 128 + 128), slice(None)),
                          scr, (slice(None), slice(None)))
        return

    E8 = p.sbuf("c_E8", [128, 8], F32)
    w8 = p.sbuf("c_w8", [128, 8], F32)
    eu8 = p.sbuf("c_eu8", [128, 8], F32)
    tmp8 = p.sbuf("c_tmp8", [128, 8], F32)
    p.op("scalar", lambda e: e.activation(out=E8[:, :], in_=sm["decay8"][:, :],
                                          func=AF.Exp),
         reads=[sm["decay8"]], writes=[E8])
    p.op("vector", lambda e: e.tensor_scalar_mul(out=tmp8[:, :], in0=E8[:, :],
                                                 scalar1=-1.0),
         reads=[E8], writes=[tmp8])
    p.op("scalar", lambda e: e.activation(out=w8[:, :], in_=tmp8[:, :], func=AF.Exp),
         reads=[tmp8], writes=[w8])
    p.op("scalar", lambda e: e.activation(out=eu8[:, :], in_=sm["first8"][:, :],
                                          func=AF.Exp),
         reads=[sm["first8"]], writes=[eu8])

    ones_t = p.sbuf("c_ones_t", [128, TC], F32)
    p.op("gpsimd", lambda e: e.memset(ones_t[:, :], 1.0), writes=[ones_t])
    wbc = p.sbuf("c_wbc", [128, TC], F32)
    aa = p.sbuf("c_aa", [128, TC], F32)
    nbuf = p.sbuf("c_nbuf", [128, TC + 1])
    dbuf = p.sbuf("c_dbuf", [128, TC + 1])
    nfin = p.sbuf("c_nfin", [128, 8], F32)
    dfin = p.sbuf("c_dfin", [128, 8], F32)

    for k in range(NCH):
        p.op("vector", lambda e, k=k: e.tensor_scalar(
            out=wbc[:, :], in0=ones_t[:, :], scalar1=w8[:, k:k + 1],
            scalar2=None, op0=ALU.mult),
            reads=[ones_t, w8], writes=[wbc])
        p.op("vector", lambda e, k=k: e.tensor_mul(
            out=aa[:, :], in0=ek[k][:, :], in1=vv[k][:, :]),
            reads=[ek[k], vv[k]], writes=[aa])
        p.op("vector", lambda e, k=k: e.tensor_tensor_scan(
            out=nbuf[:, 1:TC + 1], data0=wbc[:, :], data1=aa[:, :],
            initial=0.0, op0=ALU.mult, op1=ALU.add),
            reads=[wbc, aa], writes=[nbuf])
        p.op("vector", lambda e, k=k: e.tensor_tensor_scan(
            out=dbuf[:, 1:TC + 1], data0=wbc[:, :], data1=ek[k][:, :],
            initial=0.0, op0=ALU.mult, op1=ALU.add),
            reads=[wbc, ek[k]], writes=[dbuf])
        p.op("vector", lambda e, k=k: e.tensor_copy(
            out=nfin[:, k:k + 1], in_=nbuf[:, TC:TC + 1]),
            reads=[nbuf], writes=[nfin])
        p.op("vector", lambda e, k=k: e.tensor_copy(
            out=dfin[:, k:k + 1], in_=dbuf[:, TC:TC + 1]),
            reads=[dbuf], writes=[dfin])
    p.dma("sync", io["ag6_in"], (slice(None), slice(0, 8)), nfin,
          (slice(None), slice(None)))
    p.dma("sync", io["ag6_in"], (slice(None), slice(8, 16)), dfin,
          (slice(None), slice(None)))
    p.op("gpsimd", lambda e: e.collective_compute(
        "AllGather", ALU.bypass, replica_groups=rg,
        ins=[io["ag6_in"].h[:, :]], outs=[io["ag6_out"].h[:, :]]),
        reads=[io["ag6_in"]], writes=[io["ag6_out"]], inc=("cc", 1))
    s0n = p.sbuf("c_s0n", [128, 8], F32)
    s0d = p.sbuf("c_s0d", [128, 8], F32)
    c8 = p.sbuf("c_c8", [128, 8], F32)
    t88 = p.sbuf("c_t88", [128, 8], F32)
    fj = p.sbuf("c_fj", [128, 16], F32)
    p.op("vector", lambda e: e.memset(s0n[:, :], 0.0), writes=[s0n])
    p.op("vector", lambda e: e.memset(s0d[:, :], 0.0), writes=[s0d])
    for j in range(NCORES):
        p.dma("sync", fj, (slice(None), slice(None)), io["ag6_out"],
              (slice(j * 128, j * 128 + 128), slice(None)))
        p.op("vector", lambda e, j=j: e.tensor_scalar(
            out=c8[:, :], in0=E8[:, :], scalar1=sm["kvec"][:, j:j + 1],
            scalar2=None, op0=ALU.mult),
            reads=[E8, sm["kvec"]], writes=[c8])
        p.op("scalar", lambda e: e.activation(out=c8[:, :], in_=c8[:, :],
                                              func=AF.Exp),
             reads=[c8], writes=[c8])
        p.op("vector", lambda e: e.tensor_mul(out=t88[:, :], in0=c8[:, :],
                                              in1=fj[:, 0:8]),
             reads=[c8, fj], writes=[t88])
        p.op("vector", lambda e: e.tensor_add(out=s0n[:, :], in0=s0n[:, :],
                                              in1=t88[:, :]),
             reads=[s0n, t88], writes=[s0n])
        p.op("vector", lambda e: e.tensor_mul(out=t88[:, :], in0=c8[:, :],
                                              in1=fj[:, 8:16]),
             reads=[c8, fj], writes=[t88])
        p.op("vector", lambda e: e.tensor_add(out=s0d[:, :], in0=s0d[:, :],
                                              in1=t88[:, :]),
             reads=[s0d, t88], writes=[s0d])
    if upto == "c1_scan1":
        return

    out_all = [p.sbuf(f"c_out{k}", [128, TC], BF16) for k in range(NCH)]
    t1 = p.sbuf("c_t1", [128, TC], F32)
    t2 = p.sbuf("c_t2", [128, TC], F32)
    for k in range(NCH):
        p.op("vector", lambda e, k=k: e.tensor_scalar(
            out=wbc[:, :], in0=ones_t[:, :], scalar1=w8[:, k:k + 1],
            scalar2=None, op0=ALU.mult),
            reads=[ones_t, w8], writes=[wbc])
        p.op("vector", lambda e, k=k: e.tensor_mul(
            out=aa[:, :], in0=ek[k][:, :], in1=vv[k][:, :]),
            reads=[ek[k], vv[k]], writes=[aa])
        p.op("vector", lambda e, k=k: e.tensor_copy(
            out=nbuf[:, 0:1], in_=s0n[:, k:k + 1]),
            reads=[s0n], writes=[nbuf])
        p.op("vector", lambda e, k=k: e.tensor_copy(
            out=dbuf[:, 0:1], in_=s0d[:, k:k + 1]),
            reads=[s0d], writes=[dbuf])
        p.op("vector", lambda e, k=k: e.tensor_tensor_scan(
            out=nbuf[:, 1:TC + 1], data0=wbc[:, :], data1=aa[:, :],
            initial=nbuf[:, 0:1], op0=ALU.mult, op1=ALU.add),
            reads=[wbc, aa, nbuf], writes=[nbuf])
        p.op("vector", lambda e, k=k: e.tensor_tensor_scan(
            out=dbuf[:, 1:TC + 1], data0=wbc[:, :], data1=ek[k][:, :],
            initial=dbuf[:, 0:1], op0=ALU.mult, op1=ALU.add),
            reads=[wbc, ek[k], dbuf], writes=[dbuf])
        p.op("vector", lambda e, k=k: e.tensor_scalar(
            out=t1[:, :], in0=aa[:, :], scalar1=eu8[:, k:k + 1], scalar2=None,
            op0=ALU.mult),
            reads=[aa, eu8], writes=[t1])
        p.op("vector", lambda e: e.tensor_add(
            out=t1[:, :], in0=t1[:, :], in1=nbuf[:, 0:TC]),
            reads=[t1, nbuf], writes=[t1])
        p.op("vector", lambda e, k=k: e.tensor_scalar(
            out=t2[:, :], in0=ek[k][:, :], scalar1=eu8[:, k:k + 1], scalar2=None,
            op0=ALU.mult),
            reads=[ek[k], eu8], writes=[t2])
        p.op("vector", lambda e: e.tensor_add(
            out=t2[:, :], in0=t2[:, :], in1=dbuf[:, 0:TC]),
            reads=[t2, dbuf], writes=[t2])
        p.op("vector", lambda e: e.tensor_scalar_add(out=t2[:, :], in0=t2[:, :],
                                                     scalar1=1e-9),
             reads=[t2], writes=[t2])
        p.op("vector", lambda e: e.reciprocal(out=t2[:, :], in_=t2[:, :]),
             reads=[t2], writes=[t2])
        p.op("vector", lambda e: e.tensor_mul(out=t1[:, :], in0=t1[:, :],
                                              in1=t2[:, :]),
             reads=[t1, t2], writes=[t1])
        p.op("vector", lambda e, k=k: e.tensor_mul(
            out=out_all[k][:, :], in0=t1[:, :], in1=rsig[k][:, :]),
            reads=[t1, rsig[k]], writes=[out_all[k]])
    p.free(t1, t2, s0n, s0d, c8, t88, fj, wbc, aa, nbuf, dbuf, ones_t)
    for k in range(NCH):
        p.free(rsig[k], ek[k], vv[k])
    if upto == "c1_scan2":
        return

    sums = p.sbuf("c_sums", [128, 8], F32)
    sumq = p.sbuf("c_sumq", [128, 8], F32)
    sq = p.sbuf("c_sq", [128, TC], BF16)
    for k in range(NCH):
        p.op("vector", lambda e, k=k: e.reduce_sum(
            out=sums[:, k:k + 1], in_=out_all[k][:, :], axis=mybir.AxisListType.X),
            reads=[out_all[k]], writes=[sums])
        p.op("scalar", lambda e, k=k: e.activation(out=sq[:, :],
                                                   in_=out_all[k][:, :],
                                                   func=AF.Square),
             reads=[out_all[k]], writes=[sq])
        p.op("vector", lambda e, k=k: e.reduce_sum(
            out=sumq[:, k:k + 1], in_=sq[:, :], axis=mybir.AxisListType.X),
            reads=[sq], writes=[sumq])
    p.free(sq)
    psst = p.psum("c_psst", [2, 16], F32)
    p.mm(psst, (slice(0, 2), slice(0, 8)), sm["parf"], (slice(None), slice(None)),
         sums, (slice(None), slice(None)))
    p.mm(psst, (slice(0, 2), slice(8, 16)), sm["parf"], (slice(None), slice(None)),
         sumq, (slice(None), slice(None)))
    stat2 = p.sbuf("c_stat2", [2, 16], F32)
    p.op("vector", lambda e: e.tensor_copy(out=stat2[:, :], in_=psst[0:2, :]),
         reads=[psst], writes=[stat2])
    p.dma("sync", io["ar5_in"], (slice(None), slice(None)), stat2,
          (slice(None), slice(None)))
    p.op("gpsimd", lambda e: e.collective_compute(
        "AllReduce", ALU.add, replica_groups=rg,
        ins=[io["ar5_in"].h[:, :]], outs=[io["ar5_out"].h[:, :]]),
        reads=[io["ar5_in"]], writes=[io["ar5_out"]], inc=("cc", 1))
    p.dma("sync", stat2, (slice(None), slice(None)), io["ar5_out"],
          (slice(None), slice(None)))
    NN = float(T * 64)
    mu2 = p.sbuf("c_mu2", [2, 8], F32)
    var2 = p.sbuf("c_var2", [2, 8], F32)
    p.op("vector", lambda e: e.tensor_scalar_mul(out=mu2[:, :], in0=stat2[0:2, 0:8],
                                                 scalar1=1.0 / NN),
         reads=[stat2], writes=[mu2])
    p.op("vector", lambda e: e.tensor_scalar_mul(out=var2[:, :], in0=stat2[0:2, 8:16],
                                                 scalar1=1.0 / NN),
         reads=[stat2], writes=[var2])
    musq2 = p.sbuf("c_musq2", [2, 8], F32)
    p.op("vector", lambda e: e.tensor_mul(out=musq2[:, :], in0=mu2[:, :],
                                          in1=mu2[:, :]),
         reads=[mu2], writes=[musq2])
    p.op("vector", lambda e: e.tensor_sub(out=var2[:, :], in0=var2[:, :],
                                          in1=musq2[:, :]),
         reads=[var2, musq2], writes=[var2])
    p.op("vector", lambda e: e.tensor_scalar_add(out=var2[:, :], in0=var2[:, :],
                                                 scalar1=1e-5),
         reads=[var2], writes=[var2])
    p.op("scalar", lambda e: e.activation(out=var2[:, :], in_=var2[:, :],
                                          func=AF.Ln),
         reads=[var2], writes=[var2])
    p.op("vector", lambda e: e.tensor_scalar_mul(out=var2[:, :], in0=var2[:, :],
                                                 scalar1=-0.5),
         reads=[var2], writes=[var2])
    p.op("scalar", lambda e: e.activation(out=var2[:, :], in_=var2[:, :],
                                          func=AF.Exp),
         reads=[var2], writes=[var2])
    psb = p.psum("c_psb", [128, 4], F32)
    scol = p.sbuf("c_scol", [128, 1], F32)
    bcol = p.sbuf("c_bcol", [128, 1], F32)
    for k in range(NCH):
        p.mm(psb, (slice(None), slice(0, 1)), e2, (slice(None), slice(None)),
             var2, (slice(0, 2), slice(k, k + 1)))
        p.mm(psb, (slice(None), slice(1, 2)), e2, (slice(None), slice(None)),
             mu2, (slice(0, 2), slice(k, k + 1)))
        p.op("vector", lambda e, k=k: e.tensor_mul(
            out=scol[:, :], in0=psb[:, 0:1], in1=sm["gnw8"][:, k:k + 1]),
            reads=[psb, sm["gnw8"]], writes=[scol])
        p.op("vector", lambda e, k=k: e.tensor_mul(
            out=bcol[:, :], in0=psb[:, 1:2], in1=scol[:, :]),
            reads=[psb, scol], writes=[bcol])
        p.op("vector", lambda e, k=k: e.tensor_sub(
            out=bcol[:, :], in0=sm["gnb8"][:, k:k + 1], in1=bcol[:, :]),
            reads=[sm["gnb8"], bcol], writes=[bcol])
        p.op("vector", lambda e, k=k: e.tensor_scalar(
            out=out_all[k][:, :], in0=out_all[k][:, :], scalar1=scol[:, :],
            scalar2=bcol[:, :], op0=ALU.mult, op1=ALU.add),
            reads=[out_all[k], scol, bcol], writes=[out_all[k]])
    p.free(psst, psb, scol, bcol, mu2, var2, musq2, stat2, sums, sumq)
    if upto == "c1_gn":
        return

    wo = [p.sbuf(f"c_wo{k}", [128, 1024], BF16, hi=True) for k in range(NCH)]
    for k in range(NCH):
        p.dma("gpsimd", wo[k], (slice(None), slice(None)), io["wo"],
              (slice(k * 128, k * 128 + 128), slice(None)))
    pso = p.psum("c_pso", [128, 512], F32)
    a1b = p.sbuf("c_a1b", [128, TC], F32)
    for o in range(NCH):
        for tt in range(nt):
            tsl = slice(tt * 512, tt * 512 + 512)
            for k in range(NCH):
                p.mm(pso, (slice(None), slice(None)),
                     wo[k], (slice(None), slice(o * 128, o * 128 + 128)),
                     out_all[k], (slice(None), tsl),
                     start=(k == 0), stop=(k == NCH - 1))
            p.op("vector", lambda e, tsl=tsl: e.tensor_copy(
                out=a1b[:, tsl], in_=pso[:, :]),
                reads=[pso], writes=[a1b])
        p.dma("sync", io["a1_o"], (slice(o * 128, o * 128 + 128), slice(None)),
              a1b, (slice(None), slice(None)))
    p.free(pso, a1b)
    for k in range(NCH):
        p.free(wo[k], out_all[k])


def build_launch_C(upto="full"):
    p = Prog()
    io = {}
    io["xn1_i"] = p.din("xn1_i", [D, TC])
    for nm in ("tmr8", "tmk8", "tmv8", "decay8", "first8", "gnw8", "gnb8",
               "kvec", "selmask"):
        io[nm] = p.din(nm, [128, 8])
    io["parf"] = p.din("parf", [128, 2])
    io["e2"] = p.din("e2", [2, 128])
    for nm in ("wr", "wk", "wv", "wo"):
        io[nm] = p.din(nm, [D, D])
    io["ag4_in"] = p.dram("ag4_in", [128, 8])
    io["ag4_out"] = p.dram("ag4_out", [1024, 8])
    io["ag6_in"] = p.dram("ag6_in", [128, 16])
    io["ag6_out"] = p.dram("ag6_out", [1024, 16])
    io["ar5_in"] = p.dram("ar5_in", [2, 16])
    io["ar5_out"] = p.dram("ar5_out", [2, 16])
    io["a1_o"] = p.dout("a1_o", [D, TC])
    c = emit_consts(p)
    emit_l1_attn(p, c, io, upto=upto)
    p.finalize_output_waits()
    return p.build()


def prep_C(inputs, core, xn1):
    def col8(v):
        return np.ascontiguousarray(np.asarray(v).reshape(8, 128).T)
    kvec = np.full((128, 8), -1e9, np.float32)
    for j in range(core):
        kvec[:, j] = -2048.0 * (core - 1 - j)
    selmask = np.zeros((128, 8), np.float32)
    if core > 0:
        selmask[:, core - 1] = 1.0
    parf = np.zeros((128, 2), np.float32)
    parf[0:64, 0] = 1.0
    parf[64:128, 1] = 1.0
    e2 = np.zeros((2, 128), np.float32)
    e2[0, 0:64] = 1.0
    e2[1, 64:128] = 1.0
    return {
        "xn1_i": np.ascontiguousarray(xn1),
        "tmr8": col8(inputs["L1_attn_time_mix_r"]),
        "tmk8": col8(inputs["L1_attn_time_mix_k"]),
        "tmv8": col8(inputs["L1_attn_time_mix_v"]),
        "decay8": col8(np.asarray(inputs["L1_attn_time_decay"]).reshape(-1)),
        "first8": col8(np.asarray(inputs["L1_attn_time_first"]).reshape(-1)),
        "gnw8": col8(inputs["L1_attn_gn_weight"]),
        "gnb8": col8(inputs["L1_attn_gn_bias"]),
        "kvec": kvec,
        "selmask": selmask,
        "parf": parf,
        "e2": e2,
        "wr": np.ascontiguousarray(np.asarray(inputs["L1_attn_W_r"]).T),
        "wk": np.ascontiguousarray(np.asarray(inputs["L1_attn_W_k"]).T),
        "wv": np.ascontiguousarray(np.asarray(inputs["L1_attn_W_v"]).T),
        "wo": np.ascontiguousarray(np.asarray(inputs["L1_attn_W_o"]).T),
    }


def emit_l1_ffn(p, c, io, upto="full"):
    ncnk = TC // 128
    nt = TC // 512

    iota16 = p.sbuf("d_iota16", [128, 16], F32)
    p.dma("sync", iota16, (slice(None), slice(None)), io["iota16"],
          (slice(None), slice(None)))
    idbf = p.sbuf("d_idbf", [128, 128], BF16)
    p.dma("sync", idbf, (slice(None), slice(None)), io["idbf"],
          (slice(None), slice(None)))
    gwt = p.sbuf("d_gwt", [128, 128], F32)
    p.dma("sync", gwt, (slice(None), slice(None)), io["gwt8"],
          (slice(None), slice(None)))
    ones_row_f = p.sbuf("d_ones_row_f", [1, 128], F32)
    p.op("gpsimd", lambda e: e.memset(ones_row_f[:, :], 1.0), writes=[ones_row_f])

    xn8 = [p.sbuf(f"d_xn{k}", [128, TC]) for k in range(NCH)]
    xnb8 = [p.sbuf(f"d_xnb{k}", [128, TC], BF16) for k in range(NCH)]
    for k in range(NCH):
        p.dma("sync", xn8[k], (slice(None), slice(None)), io["xn1_i"],
              (slice(k * 128, k * 128 + 128), slice(None)))
        p.op("vector", lambda e, k=k: e.tensor_copy(out=xnb8[k][:, :], in_=xn8[k][:, :]),
             reads=[xn8[k]], writes=[xnb8[k]])

    pslog = p.psum("d_pslog", [128, 16], F32)
    logit_sb = p.sbuf("d_logit", [128, 16], F32)
    vmax = p.sbuf("d_vmax", [128, 8], F32)
    midx = p.sbuf("d_midx", [128, 8], U32)
    esub = p.sbuf("d_esub", [128, 16], F32)
    eexp = p.sbuf("d_eexp", [128, 16], F32)
    esum = p.sbuf("d_esum", [128, 1], F32)
    p1c = p.sbuf("d_p1c", [128, 1], F32)
    d21 = p.sbuf("d_d21", [128, 1], F32)
    e1f = p.sbuf("d_e1f", [128, 1], F32)
    e2f = p.sbuf("d_e2f", [128, 1], F32)
    ohh = p.sbuf("d_ohh", [128, 16], F32)
    w_all = p.sbuf("d_w_all", [128, 16 * ncnk], BF16)
    for i in range(ncnk):
        csl = slice(i * 128, i * 128 + 128)
        for k in range(NCH):
            p.mm(pslog, (slice(None), slice(None)), xn8[k], (slice(None), csl),
                 gwt, (slice(None), slice(k * 16, k * 16 + 16)),
                 start=(k == 0), stop=(k == NCH - 1))
        p.op("vector", lambda e: e.tensor_copy(out=logit_sb[:, :], in_=pslog[:, :]),
             reads=[pslog], writes=[logit_sb])
        p.op("vector", lambda e: e.max(out=vmax[:, :], in_=logit_sb[:, :]),
             reads=[logit_sb], writes=[vmax])
        p.op("vector", lambda e: e.max_index(out=midx[:, :], in_max=vmax[:, :],
                                             in_values=logit_sb[:, :]),
             reads=[logit_sb, vmax], writes=[midx])
        p.op("vector", lambda e: e.tensor_copy(out=e1f[:, :], in_=midx[:, 0:1]),
             reads=[midx], writes=[e1f])
        p.op("vector", lambda e: e.tensor_copy(out=e2f[:, :], in_=midx[:, 1:2]),
             reads=[midx], writes=[e2f])
        p.op("vector", lambda e: e.tensor_scalar(
            out=esub[:, :], in0=logit_sb[:, :], scalar1=vmax[:, 0:1], scalar2=None,
            op0=ALU.subtract),
            reads=[logit_sb, vmax], writes=[esub])
        p.op("scalar", lambda e: e.activation(out=eexp[:, :], in_=esub[:, :],
                                              func=AF.Exp),
             reads=[esub], writes=[eexp])
        p.op("vector", lambda e: e.reduce_sum(out=esum[:, :], in_=eexp[:, :],
                                              axis=mybir.AxisListType.X),
             reads=[eexp], writes=[esum])
        p.op("vector", lambda e: e.reciprocal(out=p1c[:, :], in_=esum[:, :]),
             reads=[esum], writes=[p1c])
        p.op("vector", lambda e: e.tensor_sub(out=d21[:, :], in0=vmax[:, 1:2],
                                              in1=vmax[:, 0:1]),
             reads=[vmax], writes=[d21])
        p.op("scalar", lambda e: e.activation(out=d21[:, :], in_=d21[:, :],
                                              func=AF.Exp),
             reads=[d21], writes=[d21])
        p.op("vector", lambda e: e.tensor_mul(out=d21[:, :], in0=d21[:, :],
                                              in1=p1c[:, :]),
             reads=[d21, p1c], writes=[d21])
        p.op("vector", lambda e: e.tensor_scalar(
            out=ohh[:, :], in0=iota16[:, :], scalar1=e1f[:, :], scalar2=None,
            op0=ALU.is_equal),
            reads=[iota16, e1f], writes=[ohh])
        p.op("vector", lambda e: e.tensor_scalar(
            out=ohh[:, :], in0=ohh[:, :], scalar1=p1c[:, :], scalar2=None,
            op0=ALU.mult),
            reads=[ohh, p1c], writes=[ohh])
        p.op("vector", lambda e, i=i: e.tensor_copy(
            out=w_all[:, i * 16:i * 16 + 16], in_=ohh[:, :]),
            reads=[ohh], writes=[w_all])
        p.op("vector", lambda e: e.tensor_scalar(
            out=ohh[:, :], in0=iota16[:, :], scalar1=e2f[:, :], scalar2=None,
            op0=ALU.is_equal),
            reads=[iota16, e2f], writes=[ohh])
        p.op("vector", lambda e: e.tensor_scalar(
            out=ohh[:, :], in0=ohh[:, :], scalar1=d21[:, :], scalar2=None,
            op0=ALU.mult),
            reads=[ohh, d21], writes=[ohh])
        p.op("vector", lambda e, i=i: e.scalar_tensor_tensor(
            out=w_all[:, i * 16:i * 16 + 16], in0=ohh[:, :], scalar=1.0,
            in1=w_all[:, i * 16:i * 16 + 16], op0=ALU.mult, op1=ALU.add),
            reads=[ohh, w_all], writes=[w_all])
    p.free(pslog, logit_sb, vmax, midx, esub, eexp, esum, p1c, d21, e1f, e2f, ohh)
    for k in range(NCH):
        p.free(xn8[k])
    if upto == "d_gate":
        return

    u8 = [p.sbuf(f"d_u{k}", [128, TC]) for k in range(NCH)]
    for k in range(NCH):
        p.op("vector", lambda e, k=k: e.memset(u8[k][:, :], 0.0), writes=[u8[k]])
    p.op("vector", lambda e: e.memset(u8[0][0:1, :], 1.0), writes=[u8[0]])
    alpha = p.sbuf("d_alpha", [1, TC], F32)
    p.op("vector", lambda e: e.memset(alpha[:, :], 1.0), writes=[alpha])
    wrowb = p.sbuf("d_wrowb", [1, TC], BF16)
    tn_row = p.sbuf("d_tnrow", [1, TC], F32)
    tnw_rowb = p.sbuf("d_tnwb", [1, TC], BF16)
    irow = p.sbuf("d_irow", [1, TC], F32)
    utr = tn_row
    s_t = [p.sbuf(f"d_s{f}", [128, TC], BF16) for f in range(4)]
    sw_t = s_t
    ssq = p.sbuf("d_ssq", [128, TC], BF16)
    silu_t = p.sbuf("d_silu", [128, 512], BF16)
    psum_a = p.psum("d_psa", [128, 512], F32)
    psum_b = p.psum("d_psb", [128, 512], F32)
    psum_b2 = p.psum("d_psb2", [128, 512], F32)
    psum_r = p.psum("d_psr", [1, 2048], F32)

    wbuf = []
    for buf in range(2):
        w1b = [p.sbuf(f"d_w1b{buf}_{k}", [128, 511], BF16) for k in range(NCH)]
        w3b = [p.sbuf(f"d_w3b{buf}_{k}", [128, 511], BF16) for k in range(NCH)]
        w2b = [p.sbuf(f"d_w2b{buf}_{f}", [128, 1024], BF16) for f in range(4)]
        w2t = p.sbuf(f"d_w2t{buf}", [1, 1024], BF16)
        be2r = p.sbuf(f"d_be2r{buf}", [1, 1024], BF16)
        b1c = p.sbuf(f"d_b1c{buf}", [128, 4], F32)
        b3c = p.sbuf(f"d_b3c{buf}", [128, 4], F32)
        wbuf.append(dict(w1b=w1b, w3b=w3b, w2b=w2b, w2t=w2t, be2r=be2r,
                         b1c=b1c, b3c=b3c))

    def load_expert(e_, buf):
        b = wbuf[buf]
        for k in range(NCH):
            p.dma("gpsimd", b["w1b"][k], (slice(None), slice(None)), io["we1t"],
                  (e_, slice(k * 128, k * 128 + 128), slice(None)))
            p.dma("gpsimd", b["w3b"][k], (slice(None), slice(None)), io["we3t"],
                  (e_, slice(k * 128, k * 128 + 128), slice(None)))
        for f in range(4):
            r0 = f * 128
            rows = 127 if f == 3 else 128
            p.dma("gpsimd", b["w2b"][f], (slice(0, rows), slice(None)), io["we2m"],
                  (e_, slice(r0, r0 + rows), slice(None)))
        p.dma("gpsimd", b["w2t"], (slice(None), slice(None)), io["we2t"],
              (e_, slice(0, 1), slice(None)))
        p.dma("gpsimd", b["be2r"], (slice(None), slice(None)), io["be2p"],
              (e_, slice(0, 1), slice(None)))
        p.dma("sync", b["b1c"], (slice(None), slice(None)), io["be1c"],
              (e_, slice(None), slice(None)))
        p.dma("sync", b["b3c"], (slice(None), slice(None)), io["be3c"],
              (e_, slice(None), slice(None)))

    load_expert(0, 0)

    def frows(f):
        return 127 if f == 3 else 128

    def emit_expert(e_, buf, sub=None):
        b = wbuf[buf]
        for i in range(ncnk):
            p.mm(psum_r, (slice(0, 1), slice(i * 128, i * 128 + 128)),
                 w_all, (slice(None), slice(i * 16 + e_, i * 16 + e_ + 1)),
                 idbf, (slice(None), slice(None)))
        p.op("vector", lambda e: e.tensor_mul(out=wrowb[:, :], in0=psum_r[0:1, :],
                                              in1=alpha[:, :]),
             reads=[psum_r, alpha], writes=[wrowb])
        if sub == "w":
            return
        for f in range(4):
            fr = frows(f)
            fsl = slice(f * 128, f * 128 + fr)
            for tt in range(nt):
                tsl = slice(tt * 512, tt * 512 + 512)
                for k in range(NCH):
                    p.mm(psum_a, (slice(0, fr), slice(None)),
                         b["w1b"][k], (slice(None), slice(f * 128, f * 128 + fr)),
                         xnb8[k], (slice(None), tsl),
                         start=(k == 0), stop=(k == NCH - 1))
                    p.mm(psum_b, (slice(0, fr), slice(None)),
                         b["w3b"][k], (slice(None), slice(f * 128, f * 128 + fr)),
                         xnb8[k], (slice(None), tsl),
                         start=(k == 0), stop=(k == NCH - 1))
                p.op("scalar", lambda e, fr=fr, f=f, b=b: e.activation(
                    out=silu_t[0:fr, :], in_=psum_a[0:fr, :], func=AF.Silu,
                    bias=b["b1c"][0:fr, f:f + 1]),
                    reads=[psum_a, b["b1c"]], writes=[silu_t])
                p.op("vector", lambda e, fr=fr, f=f, tsl=tsl, b=b:
                     e.scalar_tensor_tensor(
                         out=s_t[f][0:fr, tsl], in0=psum_b[0:fr, :],
                         scalar=b["b3c"][0:fr, f:f + 1], in1=silu_t[0:fr, :],
                         op0=ALU.add, op1=ALU.mult),
                     reads=[psum_b, b["b3c"], silu_t], writes=[s_t[f]])
        if sub == "s":
            return
        for f in range(4):
            fr = frows(f)
            p.op("scalar", lambda e, f=f, fr=fr: e.activation(
                out=ssq[0:fr, :], in_=s_t[f][0:fr, :], func=AF.Square),
                reads=[s_t[f]], writes=[ssq])
            for tt in range(nt):
                tsl = slice(tt * 512, tt * 512 + 512)
                p.mm(psum_r, (slice(0, 1), slice(tt * 512, tt * 512 + 512)),
                     c["ones_col"], (slice(0, fr), slice(None)),
                     ssq, (slice(0, fr), tsl),
                     start=(f == 0), stop=(f == 3))
                if f == 3:
                    p.op("vector", lambda e, tt=tt, tsl=tsl: e.tensor_scalar_add(
                        out=tn_row[0:1, tsl],
                        in0=psum_r[0:1, tt * 512:tt * 512 + 512],
                        scalar1=1.0),
                        reads=[psum_r], writes=[tn_row])
        p.op("scalar", lambda e: e.activation(out=tn_row[:, :], in_=tn_row[:, :],
                                              func=AF.Sqrt),
             reads=[tn_row], writes=[tn_row])
        if sub == "t":
            return
        p.op("vector", lambda e: e.tensor_mul(out=tnw_rowb[:, :], in0=tn_row[:, :],
                                               in1=wrowb[:, :]),
             reads=[tn_row, wrowb], writes=[tnw_rowb])
        for tt in range(nt):
            tsl = slice(tt * 512, tt * 512 + 512)
            p.mm(psum_a, (slice(None), slice(None)), c["ones_row"],
                 (slice(0, 1), slice(None)), wrowb, (slice(0, 1), tsl))
            p.op("vector", lambda e, tsl=tsl: e.tensor_copy(
                out=silu_t[:, :], in_=psum_a[:, :]),
                reads=[psum_a], writes=[silu_t])
            for f in range(4):
                fr = frows(f)
                p.op("vector", lambda e, f=f, fr=fr, tsl=tsl: e.tensor_tensor(
                    out=s_t[f][0:fr, tsl], in0=s_t[f][0:fr, tsl],
                    in1=silu_t[0:fr, :], op=ALU.mult),
                    reads=[s_t[f], silu_t], writes=[s_t[f]])
        if sub == "h":
            return
        for o in range(NCH):
            for tt in range(nt):
                tsl = slice(tt * 512, tt * 512 + 512)
                pb = psum_b if (o * nt + tt) % 2 == 0 else psum_b2
                for f in range(4):
                    fr = frows(f)
                    p.mm(pb, (slice(None), slice(None)),
                         b["w2b"][f], (slice(0, fr), slice(o * 128, o * 128 + 128)),
                         sw_t[f], (slice(0, fr), tsl),
                         start=(f == 0), stop=False)
                p.mm(pb, (slice(None), slice(None)),
                     b["w2t"], (slice(0, 1), slice(o * 128, o * 128 + 128)),
                     tnw_rowb, (slice(0, 1), tsl), start=False, stop=False)
                p.mm(pb, (slice(None), slice(None)),
                     b["be2r"], (slice(0, 1), slice(o * 128, o * 128 + 128)),
                     wrowb, (slice(0, 1), tsl), start=False, stop=True)
                p.op("scalar", lambda e, pb=pb: e.copy(
                    out=silu_t[:, :], in_=pb[:, :]),
                    reads=[pb], writes=[silu_t])
                p.op("vector", lambda e, o=o, tsl=tsl: e.tensor_add(
                    out=u8[o][:, tsl], in0=u8[o][:, tsl], in1=silu_t[:, :]),
                    reads=[u8[o], silu_t], writes=[u8[o]])
                p.op("scalar", lambda e: e.activation(
                    out=silu_t[:, :], in_=silu_t[:, :], func=AF.Square),
                    reads=[silu_t], writes=[silu_t])
                p.mm(psum_r, (slice(0, 1), slice(tt * 512, tt * 512 + 512)),
                     c["ones_col"], (slice(None), slice(None)),
                     silu_t, (slice(None), slice(None)),
                     start=(o == 0), stop=(o == NCH - 1))
        if sub in ("o", "o1", "o2", "o3", "o4"):
            return
        p.op("vector", lambda e: e.tensor_mul(out=utr[:, :], in0=wrowb[:, :],
                                              in1=wrowb[:, :]),
             reads=[wrowb], writes=[utr])
        p.op("vector", lambda e: e.tensor_add(out=irow[:, :], in0=psum_r[0:1, :],
                                              in1=utr[:, :]),
             reads=[psum_r, utr], writes=[irow])
        p.op("scalar", lambda e: e.activation(out=irow[:, :], in_=irow[:, :],
                                              func=AF.Sqrt),
             reads=[irow], writes=[irow])
        p.op("vector", lambda e: e.tensor_add(out=u8[0][0:1, :], in0=u8[0][0:1, :],
                                              in1=irow[:, :]),
             reads=[u8[0], irow], writes=[u8[0]])
        for k in range(NCH):
            p.op("scalar", lambda e, k=k: e.activation(
                out=ssq[:, :], in_=u8[k][:, :], func=AF.Square),
                reads=[u8[k]], writes=[ssq])
            for tt in range(nt):
                tsl = slice(tt * 512, tt * 512 + 512)
                p.mm(psum_r, (slice(0, 1), slice(tt * 512, tt * 512 + 512)),
                     c["ones_col"], (slice(None), slice(None)),
                     ssq, (slice(None), tsl),
                     start=(k == 0), stop=(k == NCH - 1))
                if k == NCH - 1:
                    p.op("vector", lambda e, tsl=tsl, tt=tt: e.tensor_copy(
                        out=irow[0:1, tsl],
                        in_=psum_r[0:1, tt * 512:tt * 512 + 512]),
                        reads=[psum_r], writes=[irow])
        p.op("vector", lambda e: e.tensor_mul(out=utr[:, :], in0=u8[0][0:1, :],
                                              in1=u8[0][0:1, :]),
             reads=[u8[0]], writes=[utr])
        p.op("vector", lambda e: e.scalar_tensor_tensor(
            out=irow[:, :], in0=utr[:, :], scalar=2.0,
            in1=irow[:, :], op0=ALU.mult, op1=ALU.subtract),
            reads=[utr, irow], writes=[irow])
        p.op("vector", lambda e: e.tensor_scalar_max(out=irow[:, :], in0=irow[:, :],
                                                     scalar1=1e-6),
             reads=[irow], writes=[irow])
        p.op("scalar", lambda e: e.activation(out=alpha[:, :], in_=irow[:, :],
                                              func=AF.Sqrt),
             reads=[irow], writes=[alpha])

    if upto.startswith("d_e0"):
        emit_expert(0, 0, sub=upto[4:] or None)
        return
    for e_ in range(16):
        if e_ < 15:
            load_expert(e_ + 1, (e_ + 1) % 2)
        emit_expert(e_, e_ % 2)
    if upto == "d_experts":
        return

    for bd in wbuf:
        for t in list(bd["w1b"]) + list(bd["w3b"]) + list(bd["w2b"]):
            p.free(t)
        p.free(bd["w2t"], bd["be2r"], bd["b1c"], bd["b3c"])
    for f in range(4):
        p.free(s_t[f])
    ws1 = [p.sbuf(f"d_ws1_{k}", [128, 1023], BF16) for k in range(NCH)]
    ws3 = [p.sbuf(f"d_ws3_{k}", [128, 1023], BF16) for k in range(NCH)]
    bs1c = p.sbuf("d_bs1c", [128, 8], F32)
    bs3c = p.sbuf("d_bs3c", [128, 8], F32)
    for k in range(NCH):
        p.dma("gpsimd", ws1[k], (slice(None), slice(None)), io["ws1t"],
              (slice(k * 128, k * 128 + 128), slice(None)))
        p.dma("gpsimd", ws3[k], (slice(None), slice(None)), io["ws3t"],
              (slice(k * 128, k * 128 + 128), slice(None)))
    p.dma("sync", bs1c, (slice(None), slice(None)), io["bs1c"], (slice(None), slice(None)))
    p.dma("sync", bs3c, (slice(None), slice(None)), io["bs3c"], (slice(None), slice(None)))
    ss_t = [p.sbuf(f"d_ss{f}", [128, TC], BF16) for f in range(8)]

    def sfrows(f):
        return 127 if f == 7 else 128

    p.op("vector", lambda e: e.tensor_copy(out=wrowb[:, :], in_=alpha[:, :]),
         reads=[alpha], writes=[wrowb])
    for f in range(8):
        fr = sfrows(f)
        for tt in range(nt):
            tsl = slice(tt * 512, tt * 512 + 512)
            for k in range(NCH):
                p.mm(psum_a, (slice(0, fr), slice(None)),
                     ws1[k], (slice(None), slice(f * 128, f * 128 + fr)),
                     xnb8[k], (slice(None), tsl),
                     start=(k == 0), stop=(k == NCH - 1))
                p.mm(psum_b, (slice(0, fr), slice(None)),
                     ws3[k], (slice(None), slice(f * 128, f * 128 + fr)),
                     xnb8[k], (slice(None), tsl),
                     start=(k == 0), stop=(k == NCH - 1))
            p.op("scalar", lambda e, fr=fr, f=f: e.activation(
                out=silu_t[0:fr, :], in_=psum_a[0:fr, :], func=AF.Silu,
                bias=bs1c[0:fr, f:f + 1]),
                reads=[psum_a, bs1c], writes=[silu_t])
            p.op("vector", lambda e, fr=fr, f=f, tsl=tsl: e.scalar_tensor_tensor(
                out=ss_t[f][0:fr, tsl], in0=psum_b[0:fr, :],
                scalar=bs3c[0:fr, f:f + 1], in1=silu_t[0:fr, :],
                op0=ALU.add, op1=ALU.mult),
                reads=[psum_b, bs3c, silu_t], writes=[ss_t[f]])
    for f in range(8):
        fr = sfrows(f)
        p.op("scalar", lambda e, f=f, fr=fr: e.activation(
            out=ssq[0:fr, :], in_=ss_t[f][0:fr, :], func=AF.Square),
            reads=[ss_t[f]], writes=[ssq])
        for tt in range(nt):
            tsl = slice(tt * 512, tt * 512 + 512)
            p.mm(psum_r, (slice(0, 1), slice(tt * 512, tt * 512 + 512)),
                 c["ones_col"], (slice(0, fr), slice(None)),
                 ssq, (slice(0, fr), tsl),
                 start=(f == 0), stop=(f == 7))
            if f == 7:
                p.op("vector", lambda e, tsl=tsl, tt=tt: e.tensor_scalar_add(
                    out=tn_row[0:1, tsl],
                    in0=psum_r[0:1, tt * 512:tt * 512 + 512],
                    scalar1=1.0),
                    reads=[psum_r], writes=[tn_row])
    p.op("scalar", lambda e: e.activation(out=tn_row[:, :], in_=tn_row[:, :],
                                          func=AF.Sqrt),
         reads=[tn_row], writes=[tn_row])
    p.op("vector", lambda e: e.tensor_mul(out=tnw_rowb[:, :], in0=tn_row[:, :],
                                           in1=wrowb[:, :]),
         reads=[tn_row, wrowb], writes=[tnw_rowb])
    for tt in range(nt):
        tsl = slice(tt * 512, tt * 512 + 512)
        p.mm(psum_a, (slice(None), slice(None)), c["ones_row"],
             (slice(0, 1), slice(None)), wrowb, (slice(0, 1), tsl))
        p.op("vector", lambda e, tsl=tsl: e.tensor_copy(
            out=silu_t[:, :], in_=psum_a[:, :]),
            reads=[psum_a], writes=[silu_t])
        for f in range(8):
            fr = sfrows(f)
            p.op("vector", lambda e, f=f, fr=fr, tsl=tsl: e.tensor_tensor(
                out=ss_t[f][0:fr, tsl], in0=ss_t[f][0:fr, tsl],
                in1=silu_t[0:fr, :], op=ALU.mult),
                reads=[ss_t[f], silu_t], writes=[ss_t[f]])
    for k in range(NCH):
        p.free(ws1[k], ws3[k])
    ws2 = [p.sbuf(f"d_ws2_{f}", [128, 1024], BF16) for f in range(8)]
    ws2t = p.sbuf("d_ws2t", [1, 1024], BF16)
    bs2r = p.sbuf("d_bs2r", [1, 1024], BF16)
    for f in range(8):
        r0 = f * 128
        rows = 127 if f == 7 else 128
        p.dma("gpsimd", ws2[f], (slice(0, rows), slice(None)), io["ws2m"],
              (slice(r0, r0 + rows), slice(None)))
    p.dma("gpsimd", ws2t, (slice(None), slice(None)), io["ws2t"],
          (slice(0, 1), slice(None)))
    p.dma("gpsimd", bs2r, (slice(None), slice(None)), io["bs2p"],
          (slice(0, 1), slice(None)))
    for o in range(NCH):
        for tt in range(nt):
            tsl = slice(tt * 512, tt * 512 + 512)
            pb = psum_b if (o * nt + tt) % 2 == 0 else psum_b2
            for f in range(8):
                fr = sfrows(f)
                p.mm(pb, (slice(None), slice(None)),
                     ws2[f], (slice(0, fr), slice(o * 128, o * 128 + 128)),
                     ss_t[f], (slice(0, fr), tsl),
                     start=(f == 0), stop=False)
            p.mm(pb, (slice(None), slice(None)),
                 ws2t, (slice(0, 1), slice(o * 128, o * 128 + 128)),
                 tnw_rowb, (slice(0, 1), tsl), start=False, stop=False)
            p.mm(pb, (slice(None), slice(None)),
                 bs2r, (slice(0, 1), slice(o * 128, o * 128 + 128)),
                 wrowb, (slice(0, 1), tsl), start=False, stop=True)
            p.op("scalar", lambda e, pb=pb: e.copy(
                out=silu_t[:, :], in_=pb[:, :]),
                reads=[pb], writes=[silu_t])
            p.op("vector", lambda e, o=o, tsl=tsl: e.tensor_add(
                out=u8[o][:, tsl], in0=u8[o][:, tsl], in1=silu_t[:, :]),
                reads=[u8[o], silu_t], writes=[u8[o]])
            p.op("scalar", lambda e: e.activation(
                out=silu_t[:, :], in_=silu_t[:, :], func=AF.Square),
                reads=[silu_t], writes=[silu_t])
            p.mm(psum_r, (slice(0, 1), slice(tt * 512, tt * 512 + 512)),
                 c["ones_col"], (slice(None), slice(None)),
                 silu_t, (slice(None), slice(None)),
                 start=(o == 0), stop=(o == NCH - 1))
    p.op("vector", lambda e: e.tensor_mul(out=utr[:, :], in0=wrowb[:, :],
                                          in1=wrowb[:, :]),
         reads=[wrowb], writes=[utr])
    p.op("vector", lambda e: e.tensor_add(out=irow[:, :], in0=psum_r[0:1, :],
                                          in1=utr[:, :]),
         reads=[psum_r, utr], writes=[irow])
    p.op("scalar", lambda e: e.activation(out=irow[:, :], in_=irow[:, :],
                                          func=AF.Sqrt),
         reads=[irow], writes=[irow])
    p.op("vector", lambda e: e.tensor_add(out=u8[0][0:1, :], in0=u8[0][0:1, :],
                                          in1=irow[:, :]),
         reads=[u8[0], irow], writes=[u8[0]])
    for k in range(NCH):
        p.op("scalar", lambda e, k=k: e.activation(
            out=ssq[:, :], in_=u8[k][:, :], func=AF.Square),
            reads=[u8[k]], writes=[ssq])
        for tt in range(nt):
            tsl = slice(tt * 512, tt * 512 + 512)
            p.mm(psum_r, (slice(0, 1), slice(tt * 512, tt * 512 + 512)),
                 c["ones_col"], (slice(None), slice(None)),
                 ssq, (slice(None), tsl),
                 start=(k == 0), stop=(k == NCH - 1))
            if k == NCH - 1:
                p.op("vector", lambda e, tsl=tsl, tt=tt: e.tensor_copy(
                    out=irow[0:1, tsl],
                    in_=psum_r[0:1, tt * 512:tt * 512 + 512]),
                    reads=[psum_r], writes=[irow])
    p.op("vector", lambda e: e.tensor_mul(out=utr[:, :], in0=u8[0][0:1, :],
                                          in1=u8[0][0:1, :]),
         reads=[u8[0]], writes=[utr])
    p.op("vector", lambda e: e.scalar_tensor_tensor(
        out=irow[:, :], in0=utr[:, :], scalar=2.0,
        in1=irow[:, :], op0=ALU.mult, op1=ALU.subtract),
        reads=[utr, irow], writes=[irow])
    p.op("vector", lambda e: e.tensor_scalar_max(out=irow[:, :], in0=irow[:, :],
                                                 scalar1=1e-6),
         reads=[irow], writes=[irow])
    p.op("scalar", lambda e: e.activation(out=irow[:, :], in_=irow[:, :],
                                          func=AF.Sqrt),
         reads=[irow], writes=[irow])
    p.op("vector", lambda e: e.reciprocal(out=irow[:, :], in_=irow[:, :]),
         reads=[irow], writes=[irow])
    if upto == "d_shared":
        return

    x1s = p.sbuf("d_x1s", [128, TC], F32)
    a1s = p.sbuf("d_a1s", [128, TC], F32)
    rnb = p.sbuf("d_rnb", [128, 512], F32)
    for k in range(NCH):
        p.dma("sync", x1s, (slice(None), slice(None)), io["x1_i"],
              (slice(k * 128, k * 128 + 128), slice(None)))
        p.dma("sync", a1s, (slice(None), slice(None)), io["a1_i"],
              (slice(k * 128, k * 128 + 128), slice(None)))
        p.op("vector", lambda e: e.tensor_add(out=x1s[:, :], in0=x1s[:, :],
                                              in1=a1s[:, :]),
             reads=[x1s, a1s], writes=[x1s])
        for tt in range(nt):
            tsl = slice(tt * 512, tt * 512 + 512)
            p.mm(psum_a, (slice(None), slice(None)), ones_row_f,
                 (slice(0, 1), slice(None)), irow, (slice(0, 1), tsl))
            p.op("vector", lambda e, tsl=tsl: e.tensor_copy(
                out=rnb[:, :], in_=psum_a[:, :]),
                reads=[psum_a], writes=[rnb])
            p.op("vector", lambda e, k=k, tsl=tsl: e.tensor_mul(
                out=u8[k][:, tsl], in0=u8[k][:, tsl], in1=rnb[:, :]),
                reads=[u8[k], rnb], writes=[u8[k]])
        p.op("vector", lambda e, k=k: e.tensor_add(
            out=x1s[:, :], in0=x1s[:, :], in1=u8[k][:, :]),
            reads=[x1s, u8[k]], writes=[x1s])
        p.dma("sync", io["x2_o"], (slice(k * 128, k * 128 + 128), slice(None)),
              x1s, (slice(None), slice(None)))


def build_launch_D(upto="full"):
    p = Prog()
    io = {}
    io["xn1_i"] = p.din("xn1_i", [D, TC])
    io["x1_i"] = p.din("x1_i", [D, TC])
    io["a1_i"] = p.din("a1_i", [D, TC])
    io["gwt8"] = p.din("gwt8", [128, 128])
    io["iota16"] = p.din("iota16", [128, 16])
    io["idbf"] = p.din("idbf", [128, 128], BF16)
    io["we1t"] = p.din("we1t", [16, D, 511])
    io["we3t"] = p.din("we3t", [16, D, 511])
    io["we2m"] = p.din("we2m", [16, 511, 1024])
    io["we2t"] = p.din("we2t", [16, 1, 1024])
    io["be2p"] = p.din("be2p", [16, 1, 1024])
    io["be1c"] = p.din("be1c", [16, 128, 4])
    io["be3c"] = p.din("be3c", [16, 128, 4])
    io["ws1t"] = p.din("ws1t", [D, 1023])
    io["ws3t"] = p.din("ws3t", [D, 1023])
    io["ws2m"] = p.din("ws2m", [1023, 1024])
    io["ws2t"] = p.din("ws2t", [1, 1024])
    io["bs2p"] = p.din("bs2p", [1, 1024])
    io["bs1c"] = p.din("bs1c", [128, 8])
    io["bs3c"] = p.din("bs3c", [128, 8])
    io["x2_o"] = p.dout("x2_o", [D, TC])
    c = emit_consts(p)
    emit_l1_ffn(p, c, io, upto=upto)
    p.finalize_output_waits()
    return p.build()


def prep_D(inputs, core, xn1, x1, a1):
    import ml_dtypes

    def colpad(v, nch):
        out = np.zeros(nch * 128, np.float32)
        out[:len(v)] = np.asarray(v)
        return np.ascontiguousarray(out.reshape(nch, 128).T)

    gw = np.asarray(inputs["L1_ffn_gate_w"])
    gwt8 = np.ascontiguousarray(
        gw.T.reshape(8, 128, 16).transpose(1, 0, 2).reshape(128, 128))
    we1t = np.ascontiguousarray(np.asarray(inputs["L1_ffn_We1"]).transpose(0, 2, 1))
    we3t = np.ascontiguousarray(np.asarray(inputs["L1_ffn_We3"]).transpose(0, 2, 1))
    we2 = np.asarray(inputs["L1_ffn_We2"])
    we2T = we2.transpose(0, 2, 1)
    we2p = np.zeros((16, 512, 1024), np.float32)
    we2p[:, :, 1:] = we2T
    be2 = np.asarray(inputs["L1_ffn_be2"])
    be2p = np.zeros((16, 1, 1024), np.float32)
    be2p[:, 0, 1:] = be2
    ws1t = np.ascontiguousarray(np.asarray(inputs["L1_ffn_Ws1"]).T)
    ws3t = np.ascontiguousarray(np.asarray(inputs["L1_ffn_Ws3"]).T)
    ws2 = np.asarray(inputs["L1_ffn_Ws2"])
    ws2T = ws2.T
    ws2p = np.zeros((1024, 1024), np.float32)
    ws2p[:, 1:] = ws2T
    bs2 = np.asarray(inputs["L1_ffn_bs2"])
    bs2p = np.zeros((1, 1024), np.float32)
    bs2p[0, 1:] = bs2
    be1 = np.asarray(inputs["L1_ffn_be1"])
    be3 = np.asarray(inputs["L1_ffn_be3"])
    be1c = np.stack([colpad(be1[e_], 4) for e_ in range(16)])
    be3c = np.stack([colpad(be3[e_], 4) for e_ in range(16)])
    return {
        "xn1_i": np.ascontiguousarray(xn1),
        "x1_i": np.ascontiguousarray(x1),
        "a1_i": np.ascontiguousarray(a1),
        "gwt8": gwt8,
        "iota16": np.broadcast_to(np.arange(16, dtype=np.float32), (128, 16)).copy(),
        "idbf": np.eye(128, dtype=np.float32).astype(ml_dtypes.bfloat16),
        "we1t": we1t,
        "we3t": we3t,
        "we2m": np.ascontiguousarray(we2p[:, 1:512, :]),
        "we2t": np.ascontiguousarray(we2p[:, 0:1, :]),
        "be2p": be2p,
        "be1c": be1c,
        "be3c": be3c,
        "ws1t": ws1t,
        "ws3t": ws3t,
        "ws2m": np.ascontiguousarray(ws2p[1:1024, :]),
        "ws2t": np.ascontiguousarray(ws2p[0:1, :]),
        "bs2p": bs2p,
        "bs1c": colpad(np.asarray(inputs["L1_ffn_bs1"]), 8),
        "bs3c": colpad(np.asarray(inputs["L1_ffn_bs3"]), 8),
    }


def build_launch_B(upto="full"):
    p = Prog()
    io = {}
    io["xn0_i"] = p.din("xn0_i", [D, TC])
    io["x_i"] = p.din("x_i", [D, TC])
    io["a0_i"] = p.din("a0_i", [D, TC])
    io["rwt8"] = p.din("rwt8", [128, 64])
    io["w1"] = p.din("w1", [D, 4096])
    io["w2"] = p.din("w2", [4096, D])
    io["b1_col"] = p.din("b1_col", [128, 32])
    io["b2_col"] = p.din("b2_col", [128, 8])
    io["iota8"] = p.din("iota8", [128, 8])
    io["lts"] = p.din("lts", [128, 128], BF16)
    io["idbf"] = p.din("idbf", [128, 128], BF16)
    io["ut9"] = p.din("ut9", [8, 9])
    io["i8"] = p.din("i8", [8, 8])
    io["maskrowc"] = p.din("maskrowc", [8, 8])
    io["masklte"] = p.din("masklte", [8, 8])
    io["prefcol8"] = p.din("prefcol8", [8, 1])
    io["coff_row"] = p.din("coff_row", [1, 8])
    io["p3072"] = p.din("p3072", [128, 24])
    io["send"] = p.dram("send", [TC, D], BF16)
    io["agtok"] = p.dram("agtok", [T, D], BF16)
    io["ag2_in"] = p.dram("ag2_in", [1, 8])
    io["ag2_out"] = p.dram("ag2_out", [8, 8])
    io["eo"] = p.dram("eo", [3072, D], BF16)
    io["ageo"] = p.dram("ageo", [8 * 3072, D], BF16)
    io["x1_o"] = p.dout("x1_o", [D, TC])
    io["xn1_o"] = p.dout("xn1_o", [D, TC])
    c = emit_consts(p)
    emit_l0_moe(p, c, io, upto=upto)
    p.finalize_output_waits()
    return p.build()


def prep_B(inputs, core, xn0, x_fm, a0):
    import ml_dtypes
    rw = np.asarray(inputs["L0_ffn_router_w"])
    w1 = np.asarray(inputs["L0_ffn_w1"])[core]
    w2 = np.asarray(inputs["L0_ffn_w2"])[core]
    b1 = np.asarray(inputs["L0_ffn_b1"])[core]
    b2 = np.asarray(inputs["L0_ffn_b2"])[core]
    rwt8 = np.ascontiguousarray(
        rw.T.reshape(8, 128, 8).transpose(1, 0, 2).reshape(128, 64))
    ut9 = np.zeros((8, 9), np.float32)
    for j in range(8):
        ut9[j, j + 1:] = 1.0
    maskrowc = np.zeros((8, 8), np.float32)
    maskrowc[:, core] = 1.0
    masklte = np.zeros((8, 8), np.float32)
    masklte[:, :core] = 1.0
    prefcol8 = np.zeros((8, 1), np.float32)
    prefcol8[:core] = 1.0
    coff = (np.arange(8, dtype=np.float32) * TC).reshape(1, 8)
    p3072 = (np.arange(24, dtype=np.float32)[None, :] * 128
             + np.arange(128, dtype=np.float32)[:, None])
    return {
        "xn0_i": np.ascontiguousarray(xn0),
        "x_i": np.ascontiguousarray(x_fm),
        "a0_i": np.ascontiguousarray(a0),
        "rwt8": rwt8,
        "w1": np.ascontiguousarray(w1),
        "w2": np.ascontiguousarray(w2),
        "b1_col": np.ascontiguousarray(b1.reshape(32, 128).T),
        "b2_col": np.ascontiguousarray(b2.reshape(8, 128).T),
        "iota8": np.broadcast_to(np.arange(8, dtype=np.float32), (128, 8)).copy(),
        "lts": np.triu(np.ones((128, 128), np.float32), 1).astype(ml_dtypes.bfloat16),
        "idbf": np.eye(128, dtype=np.float32).astype(ml_dtypes.bfloat16),
        "ut9": ut9,
        "i8": np.eye(8, dtype=np.float32),
        "maskrowc": maskrowc,
        "masklte": masklte,
        "prefcol8": prefcol8,
        "coff_row": coff,
        "p3072": p3072.astype(np.float32),
    }


def prep_A(inputs, core):
    x = np.asarray(inputs["L0_attn_x"])[0]
    sl = slice(core * TC, core * TC + TC)
    qkvb = np.asarray(inputs["L0_attn_qkv_b"])
    outb = np.asarray(inputs["L0_attn_out_b"])
    pref = np.zeros((128, 8), np.float32)
    pref[:, :core] = 1.0
    import ml_dtypes
    cmask = np.triu(np.ones((128, 128), np.float32)).astype(ml_dtypes.bfloat16)
    parind = np.zeros((128, 2), np.float32)
    parind[0:64, 0] = 1.0
    parind[64:128, 1] = 1.0
    return {
        "x_fm": np.ascontiguousarray(x[sl].T),
        "parind": parind.astype(ml_dtypes.bfloat16),
        "qkvw": np.ascontiguousarray(np.asarray(inputs["L0_attn_qkv_w"])),
        "qkvb_col": np.ascontiguousarray(qkvb.reshape(24, 128).T),
        "qkvb_row": np.ascontiguousarray(qkvb.reshape(1, 3072)),
        "outw": np.ascontiguousarray(np.asarray(inputs["L0_attn_out_w"])),
        "outb_col": np.ascontiguousarray(outb.reshape(8, 128).T),
        "prefmask": pref,
        "cmask": cmask,
    }


def kernel(**inputs):
    cores = list(range(NCORES))
    x = np.asarray(inputs["L0_attn_x"])[0]

    resA = run_bass_kernel_spmd(build_launch_A(),
                                [prep_A(inputs, c) for c in cores],
                                core_ids=cores)
    xn0 = [resA.results[c]["xn0_o"] for c in cores]
    a0 = [resA.results[c]["a0_o"] for c in cores]

    mapsB = [prep_B(inputs, c, xn0[c],
                    np.ascontiguousarray(x[c * TC:(c + 1) * TC].T), a0[c])
             for c in cores]
    resB = run_bass_kernel_spmd(build_launch_B(), mapsB, core_ids=cores)
    x1 = [resB.results[c]["x1_o"] for c in cores]
    xn1 = [resB.results[c]["xn1_o"] for c in cores]

    resC = run_bass_kernel_spmd(build_launch_C(),
                                [prep_C(inputs, c, xn1[c]) for c in cores],
                                core_ids=cores)
    a1 = [resC.results[c]["a1_o"] for c in cores]

    mapsD = [prep_D(inputs, c, xn1[c], x1[c], a1[c]) for c in cores]
    resD = run_bass_kernel_spmd(build_launch_D(), mapsD, core_ids=cores)

    out = np.zeros((1, T, D), np.float32)
    for c in cores:
        out[0, c * TC:(c + 1) * TC, :] = resD.results[c]["x2_o"].T
    return out
